# version 09
# baseline (speedup 1.0000x reference)
import math
import numpy as np
import ml_dtypes
from contextlib import ExitStack

import concourse.bass as bass
import concourse.mybir as mybir
from concourse.bass_utils import run_bass_kernel_spmd

F32 = mybir.dt.float32
BF16 = mybir.dt.bfloat16
I32 = mybir.dt.int32
AF = mybir.ActivationFunctionType
ALU = mybir.AluOpType
AXX = mybir.AxisListType.X

E, EC = 2048, 16
J, JC = 8192, 64
H, D = 16, 128
TOK, TT = 1024, 512
S = 8192
NCORE = 8
JG = 128
NJG = J // JG
HPC = 2

LN_EPS = 1e-5
Q_EPS = 1e-5
INV_SQRT_D = 1.0 / math.sqrt(128.0)
INV_E = 1.0 / E
INV_127 = 1.0 / 127.0

PH_FFN1 = 1
PH_ATTN = 2
PH_FFN2 = 3
N_PHASES = 3
FFN_STOP = "full"


class Ctx:
    def __init__(self, nc, stack):
        self.nc = nc
        self.stack = stack
        self.sems = {}
        self.counts = {}
        self.watermark = {}

    ALIAS = {
        "d_qk0": "dqs0", "d_qk1": "dqs1", "d_wo0": "dqs0", "d_wo1": "dqs1",
        "d_vl": "g_d", "d_kh": "d_q2s0", "d_ofm": "d_q2s1",
        "d_swap0": "dguu0", "d_swap1": "dguu1",
        "d_stg0": "d_hrd0", "d_stg1": "d_hrd1",
        "d_vst0": "dws_g0", "d_vst1": "dws_g1",
        "d_ost0": "dws_u0", "d_ost1": "dws_u1",
        "d_bv": "dws_d0",
        "qk_pe": "gu_g", "qk_ev": "sil", "rope": "h_w",
        "v_pe": "gu_u", "v_ev": "h_ok",
        "m_sq": "ln_sa", "m_mm": "ln_mm", "m_rd": "ln_mr", "m_rd2": "ln_rs",
        "m_rd3": "ln_rc", "m_bc": "ln_tp", "m_ok": "ln_xr",
        "fl_sc": "q2", "fl_ex": "hrow", "fl_mk": "hred", "fl_av": "dpe",
        "fl_fin": "dep", "fl_tr": "htr", "fl_oe": "htr2",
        "o_pe": "ln_rw", "o_ep": "ln_app",
        "ag_kqv0": "ag_g", "ag_kqv1": "ag_u", "ag_o": "ag_d",
    }

    def _r(self, name):
        return self.ALIAS.get(name, name)

    def sem(self, name):
        name = self._r(name)
        if name not in self.sems:
            self.sems[name] = self.stack.enter_context(self.nc.semaphore(name))
            self.counts[name] = 0
        return self.sems[name]

    def n(self, name):
        self.sem(name)
        return self.counts[self._r(name)]

    def inc(self, instr, name, amount=1):
        name = self._r(name)
        s = self.sem(name)
        instr.then_inc(s, amount)
        self.counts[name] += amount
        return (name, self.counts[name])

    def tok(self, name):
        name = self._r(name)
        self.sem(name)
        return (name, self.counts[name])

    def wait(self, eng, ename, token):
        name, val = token
        name = self._r(name)
        if val <= 0:
            return
        key = (ename, name)
        if self.watermark.get(key, -1) >= val:
            return
        eng.wait_ge(self.sem(name), val)
        self.watermark[key] = val

    def wv(self, eng, ename, name, val):
        self.wait(eng, ename, (name, val))

    _ASYNC_PFX = ("d_", "dws", "dgu", "dqs", "g_d", "cc", "ag_")

    def barrier(self):
        self.bar_k = getattr(self, "bar_k", 0) + 1
        tgt = 5 * self.bar_k
        bar = self.sem("bar")
        blk = self._blk
        for attr in ("sync", "tensor", "scalar", "vector", "gpsimd"):
            def body(e):
                e.sem_inc(bar, 1)
                e.wait_ge(bar, tgt)
            getattr(blk, attr)(body)


def build_program(n_phases=N_PHASES):
    nc = bass.Bass("TRN2", target_bir_lowering=False)

    x_in = nc.dram_tensor("x_in", [E, TOK], F32, kind="ExternalInput").ap()
    y_out = nc.dram_tensor("y_out", [E, TOK], F32, kind="ExternalOutput").ap()
    w_attn = {nm: nc.dram_tensor(nm, [E, E], F32, kind="ExternalInput").ap()
              for nm in ("wq", "wk", "wv", "wo")}
    b_attn = {nm: nc.dram_tensor("b_" + nm, [E, 1], F32, kind="ExternalInput").ap()
              for nm in ("wq", "wk", "wv", "wo")}
    wgT = nc.dram_tensor("wgT", [E, TOK], F32, kind="ExternalInput").ap()
    wuT = nc.dram_tensor("wuT", [E, TOK], F32, kind="ExternalInput").ap()
    wdT = nc.dram_tensor("wdT", [TOK, E], F32, kind="ExternalInput").ap()
    cosr = nc.dram_tensor("cosr", [D, TOK], F32, kind="ExternalInput").ap()
    sinr = nc.dram_tensor("sinr", [D, TOK], F32, kind="ExternalInput").ap()
    tri_in = nc.dram_tensor("tri", [128, 128], BF16, kind="ExternalInput").ap()
    ident_in = nc.dram_tensor("ident", [128, 128], BF16, kind="ExternalInput").ap()

    def idram(name, shape, dt):
        return nc.dram_tensor(name, shape, dt, kind="Internal", addr_space="Local").ap()

    h_dram = idram("h_dram", [2, JC, 128, TT], F32)
    q2_dram = idram("q2_dram", [2, JC, 128, TT], BF16)
    wsq_in = idram("wsq_in", [8], F32)
    wsq_out = idram("wsq_out", [NCORE, 8], F32)
    wq_in_d = {"g": idram("wqg_in", [E, TOK], BF16),
               "u": idram("wqu_in", [E, TOK], BF16),
               "d": idram("wqd_in", [TOK, E], BF16)}
    wq_out_d = {"g": idram("wqg_g", [NCORE, E, TOK], BF16),
                "u": idram("wqu_g", [NCORE, E, TOK], BF16),
                "d": idram("wqd_g", [NCORE, TOK, E], BF16)}
    kag_in = idram("kag_in", [H, D, TOK], BF16)
    qag_in = idram("qag_in", [H, D, TOK], BF16)
    vag_in = idram("vag_in", [H, TOK, D], BF16)
    kag_out = idram("kag_out", [2, NCORE, 8, D, TOK], BF16)
    qag_out = idram("qag_out", [2, NCORE, 8, D, TOK], BF16)
    vag_out = idram("vag_out", [2, NCORE, 8, TOK, D], BF16)
    oag_in = idram("oag_in", [HPC, D, S], BF16)
    oag_out = idram("oag_out", [NCORE, HPC, D, S], BF16)

    stack = ExitStack()
    ctx = Ctx(nc, stack)

    def sbuf(name, shape, dt):
        return stack.enter_context(nc.sbuf_tensor(name, shape, dt))

    with stack:
        x = [sbuf(f"x{i}", [128, TOK], F32) for i in range(EC)]
        xdq = sbuf("xdq", [128, EC * TOK], BF16)
        ones1f = sbuf("ones1f", [1, 128], F32)
        ones128b = sbuf("ones128b", [128, 1], BF16)
        ones128f = sbuf("ones128f", [128, 1], F32)
        tri = sbuf("tri_sb", [128, 128], BF16)
        ident = sbuf("ident_sb", [128, 128], BF16)
        identf = sbuf("identf", [128, 128], F32)
        bias_sb = sbuf("bias_sb", [128, 4 * EC], F32)
        wsb = sbuf("wsb", [128, 12], F32)
        rowA = sbuf("rowA", [1, TOK], F32)
        rowB = sbuf("rowB", [1, TOK], F32)
        rowC = sbuf("rowC", [1, TOK], F32)
        rowD = sbuf("rowD", [1, TOK], F32)
        rowT = sbuf("rowT", [1, TOK], F32)
        rowE = sbuf("rowE", [1, 128], F32)
        colA = sbuf("colA", [128, 16], F32)
        accA = sbuf("accA", [128, TOK], F32)
        accB = sbuf("accB", [128, TOK], F32)
        bcA = sbuf("bcA", [128, TOK], F32)
        bcB = sbuf("bcB", [128, TOK], F32)
        qscr = {}
        wqbf = [sbuf(f"wqbf{i}", [128, E], BF16) for i in range(2)]
        zr_col = sbuf("zr_col", [128, 4], F32)
        mneg = sbuf("mneg", [128, 2], F32)

        ps = [stack.enter_context(nc.psum_tensor(f"ps{i}", [128, 512], F32))
              for i in range(8)]

        blk = stack.enter_context(nc.Block())
        ctx._blk = blk

        class Scope:
            def __init__(self):
                self.st = ExitStack()

            def sbuf(self, name, shape, dt):
                return self.st.enter_context(nc.sbuf_tensor(name, shape, dt))

            def close(self):
                self.st.close()

        def emit_bcast(quants, row_tok):
            b_pe = ctx.n("bc_pe")
            b_dv = ctx.n("bc_dv")

            @blk.tensor
            def _(t):
                ctx.wait(t, "pe", row_tok)
                k = 0
                for row, bc in quants:
                    for hh in range(2):
                        k += 1
                        if k > 2:
                            t.wait_ge(ctx.sem("bc_dv"), b_dv + k - 2)
                        bank = ps[6 + (k % 2)]
                        mm = t.matmul(bank[:, 0:512], ones1f[0:1, 0:128],
                                      row[0:1, hh * 512:(hh + 1) * 512],
                                      start=True, stop=True)
                        ctx.inc(mm, "bc_pe")

            @blk.vector
            def _(v):
                k = 0
                for row, bc in quants:
                    for hh in range(2):
                        k += 1
                        ctx.wv(v, "dve", "bc_pe", b_pe + k)
                        bank = ps[6 + (k % 2)]
                        cp = v.tensor_copy(bc[:, hh * 512:(hh + 1) * 512], bank[:, 0:512])
                        ctx.inc(cp, "bc_dv")
            return ctx.tok("bc_dv")

        def emit_ln(src, out_bf, do_quant, src_toks):
            if not isinstance(src_toks, list):
                src_toks = [src_toks]
            """src: list of 16 [128,1024] f32; out to out_bf (xdq layout).
            Uses acc*/row*/bc* scratch. Returns apply-done token ("ln_app")."""
            if do_quant:
                accC, accD, accI, bcC = (qscr["accC"], qscr["accD"],
                                         qscr["accI"], qscr["bcC"])
            b_tr = ctx.n("ln_tr")
            b_sq = ctx.n("ln_sq")
            b_sa = ctx.n("ln_sa")
            b_mm = ctx.n("ln_mm")
            b_tp = ctx.n("ln_tp")
            b_mr = ctx.n("ln_mr")
            b_t2 = ctx.n("ln_t2")
            b_xr = ctx.n("ln_xr")
            b_rw = ctx.n("ln_rw")
            b_rc = ctx.n("ln_rc")
            b_rs = ctx.n("ln_rs")
            b_ap = ctx.n("ln_app")

            @blk.vector
            def _(v):
                for tk in src_toks:
                    ctx.wait(v, "dve", tk)
                ctx.inc(v.tensor_copy(accA[:, :], src[0][:, :]), "ln_tr")
                if do_quant:
                    ctx.inc(v.tensor_copy(accC[:, :], src[0][:, :]), "ln_tr")
                    ctx.inc(v.tensor_copy(accD[:, :], src[0][:, :]), "ln_tr")
                for c in range(1, EC):
                    ctx.wait(v, "dve", ctx.tok("ln_tr"))
                    ctx.inc(v.tensor_add(accA[:, :], accA[:, :], src[c][:, :]), "ln_tr")
                    if do_quant:
                        ctx.inc(v.tensor_max(accC[:, :], accC[:, :], src[c][:, :]), "ln_tr")
                        ctx.inc(v.tensor_tensor(accD[:, :], accD[:, :], src[c][:, :],
                                                op=ALU.min), "ln_tr")

            @blk.scalar
            def _(s):
                for tk in src_toks:
                    ctx.wait(s, "act", tk)
                for c in range(EC):
                    if c >= 1:
                        s.wait_ge(ctx.sem("ln_sq"), b_sq + c)
                    a = s.activation(out=bcA[:, :], in_=src[c][:, :], func=AF.Square)
                    ctx.inc(a, "ln_sa")

            @blk.vector
            def _(v):
                for c in range(EC):
                    ctx.wv(v, "dve", "ln_sa", b_sa + c + 1)
                    ctx.wait(v, "dve", ctx.tok("ln_sq"))
                    if c == 0:
                        i = v.tensor_copy(accB[:, :], bcA[:, :])
                    else:
                        i = v.tensor_add(accB[:, :], accB[:, :], bcA[:, :])
                    ctx.inc(i, "ln_sq")
            tok_tree = ctx.tok("ln_tr")
            tok_sq = ctx.tok("ln_sq")

            @blk.tensor
            def _(t):
                ctx.wait(t, "pe", tok_tree)
                ctx.wait(t, "pe", tok_sq)
                for hh in range(2):
                    mm = t.matmul(ps[0 + hh][0:1, 0:512], ones128f[:, 0:1],
                                  accA[:, hh * 512:(hh + 1) * 512], start=True, stop=True)
                    ctx.inc(mm, "ln_mm")
                    mm = t.matmul(ps[2 + hh][0:1, 0:512], ones128f[:, 0:1],
                                  accB[:, hh * 512:(hh + 1) * 512], start=True, stop=True)
                    ctx.inc(mm, "ln_mm")
                if do_quant:
                    for tb in range(8):
                        if tb >= 2:
                            t.wait_ge(ctx.sem("ln_mr"), b_mr + 2 * (tb - 1))
                        mm = t.transpose(ps[4 + tb % 2][:, 0:128],
                                         accC[:, tb * 128:(tb + 1) * 128], identf[:, :])
                        ctx.inc(mm, "ln_tp")
                        mm = t.transpose(ps[6 + tb % 2][:, 0:128],
                                         accD[:, tb * 128:(tb + 1) * 128], identf[:, :])
                        ctx.inc(mm, "ln_tp")

            @blk.vector
            def _(v):
                for hh in range(2):
                    ctx.wv(v, "dve", "ln_mm", b_mm + 2 * hh + 2)
                    ctx.inc(v.tensor_copy(rowA[0:1, hh * 512:(hh + 1) * 512],
                                          ps[0 + hh][0:1, 0:512]), "ln_rw")
                    ctx.inc(v.tensor_copy(rowB[0:1, hh * 512:(hh + 1) * 512],
                                          ps[2 + hh][0:1, 0:512]), "ln_rw")
                if do_quant:
                    for tb in range(8):
                        ctx.wv(v, "dve", "ln_tp", b_tp + 2 * tb + 2)
                        r = v.reduce_max(colA[:, tb:tb + 1], ps[4 + tb % 2][:, 0:128],
                                         axis=AXX)
                        ctx.inc(r, "ln_mr")
                        r = v.tensor_reduce(colA[:, 8 + tb:9 + tb],
                                            ps[6 + tb % 2][:, 0:128], axis=AXX, op=ALU.min)
                        ctx.inc(r, "ln_mr")
            tok_mr = ctx.tok("ln_mr")

            if do_quant:
                @blk.tensor
                def _(t):
                    ctx.wait(t, "pe", tok_mr)
                    for tb in range(8):
                        off = (tb % 4) * 128
                        if tb == 4:
                            t.wait_ge(ctx.sem("ln_xr"), b_xr + 2)
                        mm = t.transpose(ps[6][0:1, off:off + 128],
                                         colA[:, tb:tb + 1], identf[:, :])
                        ctx.inc(mm, "ln_t2")
                        mm = t.transpose(ps[7][0:1, off:off + 128],
                                         colA[:, 8 + tb:9 + tb], identf[:, :])
                        ctx.inc(mm, "ln_t2")

                @blk.vector
                def _(v):
                    for grp in range(2):
                        ctx.wv(v, "dve", "ln_t2", b_t2 + 8 * (grp + 1))
                        ctx.inc(v.tensor_copy(rowC[0:1, grp * 512:(grp + 1) * 512],
                                              ps[6][0:1, 0:512]), "ln_xr")
                        ctx.inc(v.tensor_copy(rowD[0:1, grp * 512:(grp + 1) * 512],
                                              ps[7][0:1, 0:512]), "ln_xr")

            @blk.vector
            def _(v):
                ctx.wait(v, "dve", ctx.tok("ln_rw"))
                if do_quant:
                    ctx.wait(v, "dve", ctx.tok("ln_xr"))

                def seq(i_):
                    ctx.wait(v, "dve", ctx.tok("ln_rc"))
                ctx.inc(v.tensor_scalar(out=rowA[0:1, :], in0=rowA[0:1, :],
                                        scalar1=INV_E, scalar2=None, op0=ALU.mult), "ln_rc")
                ctx.inc(v.tensor_scalar(out=rowB[0:1, :], in0=rowB[0:1, :],
                                        scalar1=INV_E, scalar2=None, op0=ALU.mult), "ln_rc")
                seq(0)
                ctx.inc(v.tensor_mul(rowT[0:1, :], rowA[0:1, :], rowA[0:1, :]), "ln_rc")
                seq(0)
                ctx.inc(v.tensor_sub(rowB[0:1, :], rowB[0:1, :], rowT[0:1, :]), "ln_rc")
                seq(0)
                ctx.inc(v.tensor_scalar(out=rowB[0:1, :], in0=rowB[0:1, :],
                                        scalar1=LN_EPS, scalar2=None, op0=ALU.add), "ln_rc")
                seq(0)
                ctx.inc(v.reciprocal(rowT[0:1, :], rowB[0:1, :]), "ln_rc")
            tok_rc1 = ctx.tok("ln_rc")

            @blk.scalar
            def _(s):
                ctx.wait(s, "act", tok_rc1)
                a = s.activation(out=rowT[0:1, :], in_=rowT[0:1, :], func=AF.Sqrt)
                ctx.inc(a, "ln_rs")
            tok_rs = ctx.tok("ln_rs")

            @blk.vector
            def _(v):
                ctx.wait(v, "dve", tok_rs)

                def seq(i_):
                    ctx.wait(v, "dve", ctx.tok("ln_rc"))
                if do_quant:
                    ctx.inc(v.tensor_sub(rowC[0:1, :], rowC[0:1, :], rowA[0:1, :]), "ln_rc")
                    ctx.inc(v.tensor_sub(rowD[0:1, :], rowA[0:1, :], rowD[0:1, :]), "ln_rc")
                    seq(0)
                    ctx.inc(v.tensor_max(rowC[0:1, :], rowC[0:1, :], rowD[0:1, :]), "ln_rc")
                    seq(0)
                    ctx.inc(v.tensor_mul(rowC[0:1, :], rowC[0:1, :], rowT[0:1, :]), "ln_rc")
                    seq(0)
                    ctx.inc(v.tensor_scalar(out=rowC[0:1, :], in0=rowC[0:1, :],
                                            scalar1=INV_127, scalar2=Q_EPS,
                                            op0=ALU.mult, op1=ALU.add), "ln_rc")
                    seq(0)
                    ctx.inc(v.reciprocal(rowD[0:1, :], rowC[0:1, :]), "ln_rc")
                    seq(0)
                    ctx.inc(v.tensor_mul(rowT[0:1, :], rowT[0:1, :], rowD[0:1, :]), "ln_rc")
                    seq(0)
                ctx.inc(v.tensor_mul(rowB[0:1, :], rowA[0:1, :], rowT[0:1, :]), "ln_rc")
            tok_rows = ctx.tok("ln_rc")

            quants = [(rowT, bcA), (rowB, bcB)] + ([(rowC, bcC)] if do_quant else [])
            tok_bc = emit_bcast(quants, tok_rows)

            @blk.vector
            def _(v):
                ctx.wait(v, "dve", tok_bc)

                def seq(i_):
                    ctx.wait(v, "dve", ctx.tok("ln_app"))
                for c in range(EC):
                    seq(0)
                    ctx.inc(v.tensor_mul(accA[:, :], src[c][:, :], bcA[:, :]), "ln_app")
                    seq(0)
                    if do_quant:
                        ctx.inc(v.tensor_sub(accB[:, :], accA[:, :], bcB[:, :]), "ln_app")
                        seq(0)
                        ctx.inc(v.tensor_scalar(out=accI[:, :], in0=accB[:, :],
                                                scalar1=1024.0, scalar2=None,
                                                op0=ALU.add), "ln_app")
                        seq(0)
                        ctx.inc(v.scalar_tensor_tensor(out=accC[:, :], in0=accI[:, :],
                                                       scalar=-1.0, in1=accB[:, :],
                                                       op0=ALU.mult, op1=ALU.add), "ln_app")
                        seq(0)
                        ctx.inc(v.tensor_scalar(out=accC[:, :], in0=accC[:, :],
                                                scalar1=-1023.5, scalar2=None,
                                                op0=ALU.is_ge), "ln_app")
                        seq(0)
                        ctx.inc(v.scalar_tensor_tensor(out=accD[:, :], in0=accI[:, :],
                                                       scalar=-1024.0, in1=accC[:, :],
                                                       op0=ALU.add, op1=ALU.add), "ln_app")
                        seq(0)
                        ctx.inc(v.tensor_mul(out_bf[:, c * TOK:(c + 1) * TOK],
                                             accD[:, :], bcC[:, :]), "ln_app")
                    else:
                        ctx.inc(v.tensor_sub(out_bf[:, c * TOK:(c + 1) * TOK],
                                             accA[:, :], bcB[:, :]), "ln_app")
            return ctx.tok("ln_app")

        @blk.sync
        def _(sp):
            for i in range(EC):
                ctx.inc(sp.dma_start(out=x[i][:], in_=x_in[128 * i:128 * (i + 1), :]),
                        "d_x", 16)
            ctx.inc(sp.dma_start(out=tri[:], in_=tri_in[:, :]), "d_c", 16)
            ctx.inc(sp.dma_start(out=ident[:], in_=ident_in[:, :]), "d_c", 16)
            for k, nm in enumerate(("wq", "wk", "wv", "wo")):
                ctx.inc(sp.dma_start(
                    out=bias_sb[:, k * EC:(k + 1) * EC],
                    in_=b_attn[nm].rearrange("(c p) one -> p (c one)", p=128),
                    allow_slow_non_contiguous=True), "d_c", 16)
        tok_x = ctx.tok("d_x")
        tok_c = ctx.tok("d_c")

        @blk.vector
        def _(v):
            ctx.inc(v.memset(ones1f[:], 1.0), "s_init")
            ctx.inc(v.memset(ones128b[:], 1.0), "s_init")
            ctx.inc(v.memset(ones128f[:], 1.0), "s_init")
            ctx.inc(v.memset(rowE[0:1, 0:128], 0.0), "s_init")
            ctx.wait(v, "dve", tok_c)
            ctx.inc(v.tensor_copy(identf[:, :], ident[:, :]), "s_init")
        tok_init = ctx.tok("s_init")

        wscope = Scope()
        wst = [wscope.sbuf(f"wst{i}", [128, E], F32) for i in range(2)]

        nslab = {"g": EC, "u": EC, "d": TOK // 128}
        wwid = {"g": TOK, "u": TOK, "d": E}
        wdrm = {"g": wgT, "u": wuT, "d": wdT}

        halves = {m: max(1, wwid[m] // TOK) for m in ("g", "u", "d")}

        @blk.sync
        def _(sp):
            prev = None
            for mat in ("g", "u", "d"):
                ctx.sem("wab_" + mat)
                if prev is not None:
                    ctx.wv(sp, "sp", "wab_" + prev, halves[prev] * nslab[prev])
                prev = mat
                for si in range(nslab[mat]):
                    if si >= 2:
                        ctx.wv(sp, "sp", "wab_" + mat, halves[mat] * (si - 1))
                    ctx.inc(sp.dma_start(out=wst[si % 2][:, 0:wwid[mat]],
                                         in_=wdrm[mat][128 * si:128 * (si + 1), :]),
                            f"dws_{mat}{si % 2}", 16)

        tA = wscope.sbuf("tA", [128, TOK], F32)
        tB = wscope.sbuf("tB", [128, TOK], F32)
        M23 = 8388608.0

        @blk.scalar
        def _(s):
            for mi, mat in enumerate(("g", "u", "d")):
                H_ = halves[mat]
                for si in range(nslab[mat]):
                    for hh in range(H_):
                        gidx = si * H_ + hh
                        ctx.wv(s, "act", f"dws_{mat}{si % 2}", 16 * (si // 2 + 1))
                        if gidx >= 1:
                            ctx.wv(s, "act", "ws_" + mat, gidx)
                        elif mi > 0:
                            pm = ("g", "u", "d")[mi - 1]
                            ctx.wv(s, "act", "ws_" + pm, halves[pm] * nslab[pm])
                        a = s.activation(out=tA[:, :],
                                         in_=wst[si % 2][:, hh * TOK:(hh + 1) * TOK],
                                         func=AF.Abs, scale=4096.0)
                        ctx.inc(a, "wab_" + mat)

        @blk.vector
        def _(v):
            def step(op):
                t = ctx.inc(op, "wch")
                ctx.wait(v, "dve", t)
            for mi, mat in enumerate(("g", "u", "d")):
                H_ = halves[mat]
                for si in range(nslab[mat]):
                    for hh in range(H_):
                        gidx = si * H_ + hh
                        ctx.wv(v, "dve", "wab_" + mat, gidx + 1)
                        step(v.tensor_scalar(out=tB[:, :], in0=tA[:, :],
                                             scalar1=M23, scalar2=None, op0=ALU.add))
                        step(v.tensor_scalar(out=tB[:, :], in0=tB[:, :],
                                             scalar1=-M23, scalar2=None, op0=ALU.add))
                        step(v.tensor_sub(tA[:, :], tA[:, :], tB[:, :]))
                        step(v.reduce_sum(colA[:, 6:7], tB[:, :], axis=AXX))
                        step(v.reduce_sum(colA[:, 7:8], tA[:, :], axis=AXX))
                        if gidx == 0:
                            step(v.tensor_copy(colA[:, mi:mi + 1], colA[:, 6:7]))
                            e = v.tensor_copy(colA[:, 3 + mi:4 + mi], colA[:, 7:8])
                        else:
                            step(v.tensor_add(colA[:, mi:mi + 1], colA[:, mi:mi + 1],
                                              colA[:, 6:7]))
                            e = v.tensor_add(colA[:, 3 + mi:4 + mi],
                                             colA[:, 3 + mi:4 + mi], colA[:, 7:8])
                        t = ctx.inc(e, "ws_" + mat)
                        ctx.wait(v, "dve", t)
            for mi in range(3):
                step(v.tensor_scalar(out=colA[:, 8 + mi:9 + mi], in0=colA[:, mi:mi + 1],
                                     scalar1=1.0 / 512.0, scalar2=None, op0=ALU.mult))
                step(v.tensor_scalar(out=colA[:, 6:7], in0=colA[:, 8 + mi:9 + mi],
                                     scalar1=M23, scalar2=None, op0=ALU.add))
                step(v.tensor_scalar(out=colA[:, 6:7], in0=colA[:, 6:7],
                                     scalar1=-M23, scalar2=None, op0=ALU.add))
                step(v.tensor_sub(colA[:, 11 + mi:12 + mi], colA[:, 8 + mi:9 + mi],
                                  colA[:, 6:7]))
                step(v.tensor_scalar(out=colA[:, 11 + mi:12 + mi],
                                     in0=colA[:, 11 + mi:12 + mi],
                                     scalar1=512.0, scalar2=None, op0=ALU.mult))
                e = v.tensor_copy(colA[:, 8 + mi:9 + mi], colA[:, 6:7])
                t = ctx.inc(e, "wsplit")
                ctx.wait(v, "dve", t)
        tok_ws = [ctx.tok("ws_" + m) for m in ("g", "u", "d")]
        tok_wsplit = ctx.tok("wsplit")

        @blk.tensor
        def _(t):
            ctx.wait(t, "pe", tok_init)
            ctx.wait(t, "pe", tok_wsplit)
            mm = t.matmul(ps[0][0:1, 0:11], ones128f[:, 0:1], colA[:, 3:14],
                          start=True, stop=True)
            ctx.inc(mm, "pe0")
        tok_wsmm = ctx.tok("pe0")

        @blk.vector
        def _(v):
            ctx.wait(v, "dve", tok_wsmm)
            v.tensor_copy(rowE[0:1, 0:3], ps[0][0:1, 5:8])
            t = ctx.inc(v.tensor_copy(rowE[0:1, 3:6], ps[0][0:1, 8:11]), "wch")
            ctx.wait(v, "dve", t)
            ctx.inc(v.tensor_add(rowE[0:1, 3:6], rowE[0:1, 3:6],
                                 ps[0][0:1, 0:3]), "wsrow")
        tok_wsrow = ctx.tok("wsrow")

        @blk.gpsimd
        def _(g):
            ctx.wait(g, "gps", tok_wsrow)
            ctx.wait(g, "gps", tok_init)
            ctx.inc(g.dma_start(out=wsq_in[0:8].rearrange("(one c) -> one c", one=1),
                                in_=rowE[0:1, 0:8]), "g_d", 16)
            ctx.wv(g, "gps", "g_d", ctx.n("g_d"))
            cc = g.collective_compute(
                "AllGather", ALU.bypass, replica_groups=[list(range(NCORE))],
                ins=[wsq_in[:]], outs=[wsq_out.rearrange("r c -> (r c)")[:]])
            tkcc = ctx.inc(cc, "cc")
            ctx.wait(g, "gps", tkcc)
            ctx.inc(g.dma_start(out=rowE[0:1, 16:16 + 64],
                                in_=wsq_out.rearrange("r c -> (r c)")[:]), "g_d", 16)
        tok_wsag = ctx.tok("g_d")

        @blk.vector
        def _(v):
            ctx.wait(v, "dve", tok_wsag)

            def seq(i_):
                ctx.wait(v, "dve", ctx.tok("wsc"))
            ctx.inc(v.reduce_sum(rowE[0:1, 0:8].rearrange("p (c r) -> p c r", r=8),
                                 rowE[0:1, 16:80].rearrange("p (r c) -> p c r", r=8),
                                 axis=AXX), "wsc")
            seq(0)
            ctx.inc(v.tensor_scalar(out=rowE[0:1, 8:11], in0=rowE[0:1, 0:3],
                                    scalar1=2.0 ** -27, scalar2=None,
                                    op0=ALU.mult), "wsc")
            ctx.inc(v.tensor_scalar(out=rowE[0:1, 11:14], in0=rowE[0:1, 3:6],
                                    scalar1=2.0 ** -36, scalar2=None,
                                    op0=ALU.mult), "wsc")
            seq(0)
            ctx.inc(v.tensor_add(rowE[0:1, 0:3], rowE[0:1, 8:11],
                                 rowE[0:1, 11:14]), "wsc")
            seq(0)
            ctx.inc(v.tensor_scalar(out=rowE[0:1, 0:3], in0=rowE[0:1, 0:3],
                                    scalar1=Q_EPS, scalar2=None,
                                    op0=ALU.add), "wsc")
            seq(0)
            ctx.inc(v.tensor_scalar(out=rowE[0:1, 3:4], in0=rowE[0:1, 2:3],
                                    scalar1=0.5, scalar2=None, op0=ALU.mult), "wsc")
            ctx.inc(v.tensor_scalar(out=rowE[0:1, 4:7], in0=rowE[0:1, 0:3],
                                    scalar1=0.4999996423721313, scalar2=None,
                                    op0=ALU.mult), "wsc")
            seq(0)
            ctx.inc(v.tensor_scalar(out=rowE[0:1, 7:10], in0=rowE[0:1, 4:7],
                                    scalar1=-1.0, scalar2=None, op0=ALU.mult), "wsc")
        tok_wsc = ctx.tok("wsc")

        @blk.tensor
        def _(t):
            ctx.wait(t, "pe", tok_wsc)
            mm = t.matmul(ps[1][:, 0:10], ones1f[0:1, 0:128], rowE[0:1, 0:10],
                          start=True, stop=True)
            ctx.inc(mm, "pe0")
        tok_wsbmm = ctx.tok("pe0")

        @blk.vector
        def _(v):
            ctx.wait(v, "dve", tok_wsbmm)
            ctx.inc(v.tensor_copy(wsb[:, 0:10], ps[1][:, 0:10]), "wsbok")
        tok_wsb = ctx.tok("wsbok")

        wq_reqs = []

        @blk.sync
        def _(sp):
            for tk in tok_ws:
                ctx.wait(sp, "sp", tk)
            prevq = None
            for mat in ("g", "u", "d"):
                b_q = ctx.n("wq_" + mat)
                if prevq is not None:
                    pper = 3 if wwid[prevq] <= TOK else 6
                    ctx.wv(sp, "sp", "wq_" + prevq, pper * nslab[prevq])
                prevq = mat
                for si in range(nslab[mat]):
                    if si >= 2:
                        ctx.wv(sp, "sp", "wq_" + mat, b_q + per * (si - 1) if False
                               else b_q + (3 if wwid[mat] <= TOK else 6) * (si - 1))
                    ctx.inc(sp.dma_start(out=wst[si % 2][:, 0:wwid[mat]],
                                         in_=wdrm[mat][128 * si:128 * (si + 1), :]),
                            f"dws_{mat}{si % 2}", 16)

        b_dqs = [ctx.n("dqs0"), ctx.n("dqs1")]

        @blk.vector
        def _(v):
            ctx.wait(v, "dve", tok_wsb)
            gidx = 0
            for mi, mat in enumerate(("g", "u", "d")):
                w = wwid[mat]
                for si in range(nslab[mat]):
                    buf = wst[si % 2]
                    ctx.wv(v, "dve", f"dws_{mat}{si % 2}",
                           16 * (nslab[mat] + si) // 32 * 16 if False else
                           16 * (nslab[mat] // 2 + (si // 2) + 1))
                    if gidx >= 2:
                        ctx.wv(v, "dve", f"dqs{gidx % 2}",
                               b_dqs[gidx % 2] + 16 * (gidx // 2))
                    for hh in range(max(1, w // TOK)):
                        sl = slice(hh * TOK, hh * TOK + min(TOK, w))
                        ctx.wait(v, "dve", ctx.tok("wq_" + mat))
                        t1 = ctx.inc(v.tensor_scalar(out=accA[:, 0:min(TOK, w)],
                                                     in0=buf[:, sl],
                                                     scalar1=wsb[:, 4 + mi:5 + mi],
                                                     scalar2=None, op0=ALU.is_gt),
                                     "wq_" + mat)
                        t2 = ctx.inc(v.tensor_scalar(out=accB[:, 0:min(TOK, w)],
                                                     in0=buf[:, sl],
                                                     scalar1=wsb[:, 7 + mi:8 + mi],
                                                     scalar2=None, op0=ALU.is_lt),
                                     "wq_" + mat)
                        ctx.wait(v, "dve", t2)
                        q = v.tensor_sub(wqbf[gidx % 2][:, sl], accA[:, 0:min(TOK, w)],
                                         accB[:, 0:min(TOK, w)])
                        tq = ctx.inc(q, "wq_" + mat)
                    wq_reqs.append((mat, si, tq))
                    gidx += 1

        @blk.gpsimd
        def _(g):
            for gi, (mat, si, tq) in enumerate(wq_reqs):
                ctx.wait(g, "gps", tq)
                ctx.inc(g.dma_start(out=wq_in_d[mat][128 * si:128 * (si + 1), :],
                                    in_=wqbf[gi % 2][:, 0:wwid[mat]]), f"dqs{gi % 2}", 16)
            ctx.wv(g, "gps", "dqs0", ctx.n("dqs0"))
            ctx.wv(g, "gps", "dqs1", ctx.n("dqs1"))
            for mat in ("g", "u", "d"):
                cc = g.collective_compute(
                    "AllGather", ALU.bypass, replica_groups=[list(range(NCORE))],
                    ins=[wq_in_d[mat].rearrange("a b -> (a b)")[:]],
                    outs=[wq_out_d[mat].rearrange("r a b -> (r a b)")[:]])
                ctx.inc(cc, "ag_" + mat)
        tok_ag = {m: ctx.tok("ag_" + m) for m in ("g", "u", "d")}

        ctx.barrier()
        wscope.close()

        def ffn_pass(pid):
            stop = FFN_STOP
            fsc = Scope()
            qscr["accC"] = fsc.sbuf(f"accC_{pid}", [128, TOK], F32)
            qscr["accD"] = fsc.sbuf(f"accD_{pid}", [128, TOK], F32)
            qscr["accI"] = fsc.sbuf(f"accI_{pid}", [128, TOK], I32)
            qscr["bcC"] = fsc.sbuf(f"bcC_{pid}", [128, TOK], F32)
            accC, accD, accI, bcC = (qscr["accC"], qscr["accD"],
                                     qscr["accI"], qscr["bcC"])
            wslab = [fsc.sbuf(f"wslab{i}_{pid}", [128, EC * JG], BF16) for i in range(4)]
            hgrp = fsc.sbuf(f"hgrp_{pid}", [128, TOK], F32)
            sg_t = fsc.sbuf(f"sg_t_{pid}", [128, TT], F32)
            m_run = fsc.sbuf(f"m_run_{pid}", [128, TOK], F32)
            m_run2 = fsc.sbuf(f"m_run2_{pid}", [128, TOK], F32)
            hrd = [fsc.sbuf(f"hrd{i}_{pid}", [128, TT], F32) for i in range(2)]
            dwslab = [fsc.sbuf(f"dwslab{i}_{pid}", [128, 1024], BF16) for i in range(4)]
            q2sl = [fsc.sbuf(f"q2sl{i}_{pid}", [128, TT], BF16) for i in range(4)]

            tok_xq = emit_ln(x, xdq, True,
                             [ctx.tok("d_x"), ctx.tok("dep"), ctx.tok("o_ep")])
            ctx.barrier()
            if stop == "xq":
                fsc.close()
                return

            wgu_g = {"g": wq_out_d["g"], "u": wq_out_d["u"]}
            wsl = {"g": (wslab[0], wslab[1]), "u": (wslab[2], wslab[3])}
            b_dgu = {m + s: ctx.n(f"dgu{m}{s}") for m in ("g", "u") for s in ("0", "1")}
            b_gu = ctx.n("gu_g")
            b_guu = ctx.n("gu_u")
            b_sil = ctx.n("sil")
            b_h = ctx.n("h_ok")
            b_hw = ctx.n("h_w")
            b_hsp = [ctx.n("d_hsp0"), ctx.n("d_hsp1")]

            @blk.sync
            def _(sp):
                for mat in ("g", "u"):
                    ctx.wait(sp, "sp", tok_ag[mat])
                for jg in range(NJG):
                    r, pp = jg // 8, jg % 8
                    for mat in ("g", "u"):
                        if jg >= 2:
                            ctx.wv(sp, "sp", "gu_u", b_guu + 2 * (jg - 1))
                        src = wgu_g[mat][r][:, pp * JG:(pp + 1) * JG]
                        ctx.inc(sp.dma_start(
                            out=wsl[mat][jg % 2][:, :].rearrange(
                                "p (ic j) -> p ic j", j=JG),
                            in_=src.rearrange("(ic p) j -> p ic j", p=128)),
                            f"dgu{mat}{jg % 2}", 16)

            @blk.tensor
            def _(t):
                ctx.wait(t, "pe", tok_xq)
                for jg in range(NJG):
                    for tt2 in range(2):
                        unit = jg * 2 + tt2
                        bg = ps[(unit % 2) * 2]
                        bu = ps[(unit % 2) * 2 + 1]
                        ctx.wv(t, "pe", f"dgug{jg % 2}",
                               b_dgu["g" + str(jg % 2)] + 16 * (jg // 2 + 1))
                        ctx.wv(t, "pe", f"dguu{jg % 2}",
                               b_dgu["u" + str(jg % 2)] + 16 * (jg // 2 + 1))
                        if unit >= 2:
                            t.wait_ge(ctx.sem("h_ok"), b_h + 2 + 2 * (unit - 1))
                        for ic in range(EC):
                            mm = t.matmul(bg[:, 0:TT],
                                          wsl["g"][jg % 2][:, ic * JG:(ic + 1) * JG],
                                          xdq[:, ic * TOK + tt2 * TT:ic * TOK + tt2 * TT + TT],
                                          start=(ic == 0), stop=(ic == EC - 1))
                        ctx.inc(mm, "gu_g")
                        for ic in range(EC):
                            mm = t.matmul(bu[:, 0:TT],
                                          wsl["u"][jg % 2][:, ic * JG:(ic + 1) * JG],
                                          xdq[:, ic * TOK + tt2 * TT:ic * TOK + tt2 * TT + TT],
                                          start=(ic == 0), stop=(ic == EC - 1))
                        ctx.inc(mm, "gu_u")

            @blk.scalar
            def _(s):
                for jg in range(NJG):
                    for tt2 in range(2):
                        unit = jg * 2 + tt2
                        bg = ps[(unit % 2) * 2]
                        ctx.wv(s, "act", "gu_g", b_gu + unit + 1)
                        if unit >= 1:
                            s.wait_ge(ctx.sem("h_w"), b_hw + unit)
                        a = s.activation(out=sg_t[:, 0:TT], in_=bg[:, 0:TT],
                                         func=AF.Silu, scale=wsb[:, 0:1])
                        ctx.inc(a, "sil")

            @blk.vector
            def _(v):
                ctx.inc(v.memset(m_run[:, :], 0.0), "h_ok")
                ctx.inc(v.memset(m_run2[:, :], 0.0), "h_ok")
                b_h2 = b_h + 1
                for jg in range(NJG):
                    for tt2 in range(2):
                        unit = jg * 2 + tt2
                        bu = ps[(unit % 2) * 2 + 1]
                        ctx.wv(v, "dve", "sil", b_sil + unit + 1)
                        ctx.wv(v, "dve", "gu_u", b_guu + unit + 1)
                        if jg >= 1:
                            ctx.wv(v, "dve", f"d_hsp{tt2}", b_hsp[tt2] + 16 * jg)
                        hsl = hgrp[:, tt2 * TT:(tt2 + 1) * TT]
                        hh_ = v.scalar_tensor_tensor(out=hsl, in0=bu[:, 0:TT],
                                                     scalar=wsb[:, 1:2], in1=sg_t[:, 0:TT],
                                                     op0=ALU.mult, op1=ALU.mult)
                        th = ctx.inc(hh_, "h_w")
                        ctx.wait(v, "dve", th)
                        ctx.wait(v, "dve", ctx.tok("h_ok"))
                        mr = v.tensor_max(m_run[:, tt2 * TT:(tt2 + 1) * TT],
                                          m_run[:, tt2 * TT:(tt2 + 1) * TT], hsl)
                        ctx.inc(mr, "h_ok")
                        mr2 = v.tensor_tensor(m_run2[:, tt2 * TT:(tt2 + 1) * TT],
                                              m_run2[:, tt2 * TT:(tt2 + 1) * TT], hsl,
                                              op=ALU.min)
                        ctx.inc(mr2, "h_ok")

            @blk.gpsimd
            def _(g):
                for jg in range(NJG):
                    for tt2 in range(2):
                        unit = jg * 2 + tt2
                        ctx.wv(g, "gps", "h_ok", b_h + 2 + 2 * (unit + 1))
                        ctx.inc(g.dma_start(out=h_dram[tt2, jg],
                                            in_=hgrp[:, tt2 * TT:(tt2 + 1) * TT]),
                                f"d_hsp{tt2}", 16)
            tok_mrun = ctx.tok("h_ok")
            tok_hsp = [ctx.tok("d_hsp0"), ctx.tok("d_hsp1")]
            ctx.barrier()
            if stop == "gu":
                fsc.close()
                return

            @blk.vector
            def _(v):
                t1 = ctx.inc(v.tensor_scalar(out=m_run2[:, :], in0=m_run2[:, :],
                                             scalar1=-1.0, scalar2=None,
                                             op0=ALU.mult), "h_ok")
                ctx.wait(v, "dve", t1)
                ctx.inc(v.tensor_max(m_run[:, :], m_run[:, :], m_run2[:, :]), "h_ok")
            tok_mcmb = ctx.tok("h_ok")
            ctx.barrier()

            b_htr = ctx.n("htr")
            b_hred = ctx.n("hred")
            b_htr2 = ctx.n("htr2")
            b_hrow = ctx.n("hrow")

            @blk.tensor
            def _(t):
                ctx.wait(t, "pe", tok_mcmb)
                for tb in range(8):
                    if tb >= 2:
                        t.wait_ge(ctx.sem("hred"), b_hred + tb - 1)
                    mm = t.transpose(ps[4 + tb % 2][:, 0:128],
                                     m_run[:, tb * 128:(tb + 1) * 128], identf[:, :])
                    ctx.inc(mm, "htr")

            @blk.vector
            def _(v):
                for tb in range(8):
                    ctx.wv(v, "dve", "htr", b_htr + tb + 1)
                    ctx.inc(v.reduce_max(colA[:, tb:tb + 1], ps[4 + tb % 2][:, 0:128],
                                         axis=AXX), "hred")
            tok_hred = ctx.tok("hred")

            @blk.tensor
            def _(t):
                ctx.wait(t, "pe", tok_hred)
                for tb in range(8):
                    off = (tb % 4) * 128
                    if tb == 4:
                        t.wait_ge(ctx.sem("hrow"), b_hrow + 1)
                    ctx.inc(t.transpose(ps[6][0:1, off:off + 128],
                                        colA[:, tb:tb + 1], identf[:, :]), "htr2")

            @blk.vector
            def _(v):
                for grp in range(2):
                    ctx.wv(v, "dve", "htr2", b_htr2 + 4 * (grp + 1))
                    ctx.inc(v.tensor_copy(rowC[0:1, grp * 512:(grp + 1) * 512],
                                          ps[6][0:1, 0:512]), "hrow")
                ctx.wait(v, "dve", ctx.tok("hrow"))
                ctx.inc(v.tensor_scalar(out=rowC[0:1, :], in0=rowC[0:1, :],
                                        scalar1=INV_127, scalar2=Q_EPS,
                                        op0=ALU.mult, op1=ALU.add), "hrow")
                ctx.wait(v, "dve", ctx.tok("hrow"))
                ctx.inc(v.reciprocal(rowD[0:1, :], rowC[0:1, :]), "hrow")
            tok_h2 = emit_bcast([(rowC, bcA), (rowD, bcB)], ctx.tok("hrow"))
            if stop == "hscale":
                ctx.barrier()
                fsc.close()
                return

            b_hrd = [ctx.n("d_hrd0"), ctx.n("d_hrd1")]
            b_q2 = ctx.n("q2")
            b_q2s = [ctx.n("d_q2s0"), ctx.n("d_q2s1")]

            @blk.sync
            def _(sp):
                for tk in tok_hsp:
                    ctx.wait(sp, "sp", tk)
                for k in range(2 * JC):
                    tt2, jc = k // JC, k % JC
                    if k >= 2:
                        ctx.wv(sp, "sp", "q2", b_q2 + 6 * (k - 1))
                    ctx.inc(sp.dma_start(out=hrd[k % 2][:, :], in_=h_dram[tt2, jc]),
                            f"d_hrd{k % 2}", 16)

            @blk.vector
            def _(v):
                ctx.wait(v, "dve", tok_h2)
                for k in range(2 * JC):
                    tt2 = k // JC
                    bA = bcA[:, tt2 * TT:(tt2 + 1) * TT]
                    bB = bcB[:, tt2 * TT:(tt2 + 1) * TT]
                    buf = hrd[k % 2]
                    ctx.wv(v, "dve", f"d_hrd{k % 2}", b_hrd[k % 2] + 16 * (k // 2 + 1))
                    if k >= 2:
                        ctx.wv(v, "dve", f"d_q2s{k % 2}", b_q2s[k % 2] + 16 * (k // 2))
                    t1 = ctx.inc(v.tensor_mul(accA[:, 0:TT], buf[:, :], bB), "q2")
                    ctx.wait(v, "dve", t1)
                    t2 = ctx.inc(v.tensor_scalar(out=accI[:, 0:TT], in0=accA[:, 0:TT],
                                                 scalar1=1024.0, scalar2=None,
                                                 op0=ALU.add), "q2")
                    ctx.wait(v, "dve", t2)
                    t3 = ctx.inc(v.scalar_tensor_tensor(out=accB[:, 0:TT], in0=accI[:, 0:TT],
                                                        scalar=-1.0, in1=accA[:, 0:TT],
                                                        op0=ALU.mult, op1=ALU.add), "q2")
                    ctx.wait(v, "dve", t3)
                    t4 = ctx.inc(v.tensor_scalar(out=accB[:, 0:TT], in0=accB[:, 0:TT],
                                                 scalar1=-1023.5, scalar2=None,
                                                 op0=ALU.is_ge), "q2")
                    ctx.wait(v, "dve", t4)
                    t5 = ctx.inc(v.scalar_tensor_tensor(out=accC[:, 0:TT], in0=accI[:, 0:TT],
                                                        scalar=-1024.0, in1=accB[:, 0:TT],
                                                        op0=ALU.add, op1=ALU.add), "q2")
                    ctx.wait(v, "dve", t5)
                    ctx.inc(v.tensor_mul(wqbf[k % 2][:, 0:TT], accC[:, 0:TT], bA), "q2")

            @blk.gpsimd
            def _(g):
                for k in range(2 * JC):
                    tt2, jc = k // JC, k % JC
                    ctx.wv(g, "gps", "q2", b_q2 + 6 * (k + 1))
                    ctx.inc(g.dma_start(out=q2_dram[tt2, jc], in_=wqbf[k % 2][:, 0:TT]),
                            f"d_q2s{k % 2}", 16)
            tok_q2s = [ctx.tok("d_q2s0"), ctx.tok("d_q2s1")]
            ctx.barrier()
            if stop == "q2":
                fsc.close()
                return

            b_ddw = [ctx.n(f"d_dw{i}") for i in range(4)]
            b_dq2 = [ctx.n(f"d_q2r{i}") for i in range(4)]
            b_dpe = ctx.n("dpe")
            b_dep = ctx.n("dep")

            @blk.sync
            def _(sp):
                ctx.wait(sp, "sp", tok_ag["d"])
                for sw in range(4):
                    tt2, hh = sw // 2, sw % 2
                    for jc in range(JC):
                        k = sw * JC + jc
                        r, pp = jc // 8, jc % 8
                        if k >= 4:
                            ctx.wv(sp, "sp", "dpe", b_dpe + (k - 3))
                        src = wq_out_d["d"][r][pp * 128:(pp + 1) * 128,
                                               hh * 1024:(hh + 1) * 1024]
                        ctx.inc(sp.dma_start(out=dwslab[k % 4][:, :], in_=src),
                                f"d_dw{k % 4}", 16)

            @blk.gpsimd
            def _(g):
                for tk in tok_q2s:
                    ctx.wait(g, "gps", tk)
                for sw in range(4):
                    tt2 = sw // 2
                    for jc in range(JC):
                        k = sw * JC + jc
                        if k >= 4:
                            ctx.wv(g, "gps", "dpe", b_dpe + (k - 3))
                        ctx.inc(g.dma_start(out=q2sl[k % 4][:, :], in_=q2_dram[tt2, jc]),
                                f"d_q2r{k % 4}", 16)

            @blk.tensor
            def _(t):
                for sw in range(4):
                    tt2, hh = sw // 2, sw % 2
                    for jc in range(JC):
                        k = sw * JC + jc
                        ctx.wv(t, "pe", f"d_q2r{k % 4}", b_dq2[k % 4] + 16 * (k // 4 + 1))
                        ctx.wv(t, "pe", f"d_dw{k % 4}", b_ddw[k % 4] + 16 * (k // 4 + 1))
                        if jc == 0 and sw >= 1:
                            t.wait_ge(ctx.sem("dep"), b_dep + 8 * sw)
                        mm = None
                        for ho in range(8):
                            mm = t.matmul(ps[ho][:, 0:TT],
                                          dwslab[k % 4][:, ho * 128:(ho + 1) * 128],
                                          q2sl[k % 4][:, :],
                                          start=(jc == 0), stop=(jc == JC - 1))
                        ctx.inc(mm, "dpe")

            @blk.vector
            def _(v):
                for sw in range(4):
                    tt2, hh = sw // 2, sw % 2
                    ctx.wv(v, "dve", "dpe", b_dpe + JC * (sw + 1))
                    for ho in range(8):
                        ch = hh * 8 + ho
                        e = v.scalar_tensor_tensor(
                            out=x[ch][:, tt2 * TT:(tt2 + 1) * TT],
                            in0=ps[ho][:, 0:TT], scalar=wsb[:, 3:4],
                            in1=x[ch][:, tt2 * TT:(tt2 + 1) * TT],
                            op0=ALU.mult, op1=ALU.add)
                        ctx.inc(e, "dep")
            ctx.barrier()
            fsc.close()

        def attn_phase():
            qsc = Scope()
            cos_sb = qsc.sbuf("cos_sb", [128, TOK], F32)
            sin_sb = qsc.sbuf("sin_sb", [128, TOK], F32)
            qkslab = [qsc.sbuf(f"qkslab{i}", [128, EC * 128], BF16) for i in range(2)]
            vslab = qsc.sbuf("vslab", [128, EC * 512], BF16)
            att_st = qsc.sbuf("att_st", [128, TOK], F32)
            att_sw = qsc.sbuf("att_sw", [128, TOK], F32)
            att_bf = qsc.sbuf("att_bf", [128, TOK], BF16)
            vevac = qsc.sbuf("vevac", [128, TT], BF16)
            bvb = qsc.sbuf("bvb", [128, E], BF16)

            @blk.sync
            def _(sp):
                ctx.inc(sp.dma_start(out=cos_sb[:], in_=cosr[:, :]), "d_c", 16)
                ctx.inc(sp.dma_start(out=sin_sb[:], in_=sinr[:, :]), "d_c", 16)
            tok_cs = ctx.tok("d_c")

            tok_ln2 = emit_ln(x, xdq, False, [ctx.tok("d_x"), ctx.tok("dep")])
            ctx.barrier()

            @blk.sync
            def _(sp):
                ctx.inc(sp.dma_start(out=rowA[0:1, :],
                                     in_=b_attn["wv"][0:TOK].rearrange("a one -> one a")),
                        "d_bv", 16)
                ctx.inc(sp.dma_start(out=rowB[0:1, :],
                                     in_=b_attn["wv"][TOK:E].rearrange("a one -> one a")),
                        "d_bv", 16)
            tok_bvr = ctx.tok("d_bv")
            b_bv = ctx.n("bc_dv")

            @blk.tensor
            def _(t):
                ctx.wait(t, "pe", tok_bvr)
                for q in range(4):
                    if q >= 2:
                        t.wait_ge(ctx.sem("bc_dv"), b_bv + q - 1)
                    row = rowA if q < 2 else rowB
                    mm = t.matmul(ps[6 + (q % 2)][:, 0:512], ones1f[0:1, 0:128],
                                  row[0:1, (q % 2) * 512:(q % 2 + 1) * 512],
                                  start=True, stop=True)
                    ctx.inc(mm, "bc_pe")
            b_bvp = ctx.n("bc_pe")

            @blk.vector
            def _(v):
                for q in range(4):
                    ctx.wv(v, "dve", "bc_pe", b_bvp - 4 + q + 1)
                    ctx.inc(v.tensor_copy(bvb[:, q * 512:(q + 1) * 512],
                                          ps[6 + (q % 2)][:, 0:512]), "bc_dv")
            tok_bv = ctx.tok("bc_dv")

            for mat_i, (mat, wdr, bcol, stage_d) in enumerate(
                    (("q", w_attn["wq"], 0, qag_in), ("k", w_attn["wk"], 1, kag_in))):
                b_wl = [ctx.n("d_qk0"), ctx.n("d_qk1")]
                b_pe = ctx.n("qk_pe")
                b_ev = ctx.n("qk_ev")
                b_sw = [ctx.n("d_swap0"), ctx.n("d_swap1")]
                b_ro = ctx.n("rope")
                b_st = [ctx.n("d_stg0"), ctx.n("d_stg1")]

                @blk.gpsimd
                def _(g):
                    for h in range(H):
                        hh = mat_i * H + h
                        if h >= 2:
                            ctx.wv(g, "gps", "qk_pe", b_pe + 2 * (h - 1))
                        elif mat_i == 1:
                            ctx.wv(g, "gps", "qk_pe", b_pe)
                        src = wdr[:, h * 128:(h + 1) * 128]
                        ctx.inc(g.dma_start(
                            out=qkslab[hh % 2][:, :].rearrange("p (ic e) -> p ic e", e=128),
                            in_=src.rearrange("(ic p) e -> p ic e", p=128)),
                            f"d_qk{hh % 2}", 16)

                @blk.tensor
                def _(t):
                    ctx.wait(t, "pe", ctx.tok("ln_app"))
                    for h in range(H):
                        hh = mat_i * H + h
                        ctx.wv(t, "pe", f"d_qk{h % 2}",
                               b_wl[h % 2] + 16 * (h // 2 + 1))
                        if h >= 1:
                            t.wait_ge(ctx.sem("qk_ev"), b_ev + 2 * h)
                        elif mat_i == 1:
                            ctx.wv(t, "pe", "qk_ev", b_ev)
                        for tt2 in range(2):
                            for ic in range(EC):
                                mm = t.matmul(
                                    ps[tt2][:, 0:TT],
                                    qkslab[hh % 2][:, ic * 128:(ic + 1) * 128],
                                    xdq[:, ic * TOK + tt2 * TT:ic * TOK + tt2 * TT + TT],
                                    start=(ic == 0), stop=(ic == EC - 1))
                            ctx.inc(mm, "qk_pe")

                @blk.scalar
                def _(s):
                    for h in range(H):
                        hh = mat_i * H + h
                        if h >= 1:
                            s.wait_ge(ctx.sem("rope"), b_ro + 3 * (h - 1) + 2)
                            ctx.wv(s, "act", "rope", b_ro + 3 * h)
                            ctx.inc(s.dma_start(out=stage_d[h - 1], in_=att_bf[:, :]),
                                    f"d_stg{(h - 1) % 2}", 16)
                        for tt2 in range(2):
                            ctx.wv(s, "act", "qk_pe", b_pe + 2 * h + tt2 + 1)
                            a = s.activation(out=att_st[:, tt2 * TT:(tt2 + 1) * TT],
                                             in_=ps[tt2][:, 0:TT], func=AF.Identity,
                                             bias=bias_sb[:, bcol * EC + h:bcol * EC + h + 1],
                                             scale=1.0)
                            ctx.inc(a, "qk_ev")
                    ctx.wv(s, "act", "rope", b_ro + 3 * H)
                    ctx.inc(s.dma_start(out=stage_d[H - 1], in_=att_bf[:, :]),
                            f"d_stg{(H - 1) % 2}", 16)

                @blk.sync
                def _(sp):
                    for h in range(H):
                        ctx.wv(sp, "sp", "qk_ev", b_ev + 2 * (h + 1))
                        if h >= 1:
                            ctx.wv(sp, "sp", "rope", b_ro + 3 * h)
                        ctx.inc(sp.dma_start(out=att_sw[0:64, :], in_=att_st[64:128, :]),
                                f"d_swap{h % 2}", 16)
                        ctx.inc(sp.dma_start(out=att_sw[64:128, :], in_=att_st[0:64, :]),
                                f"d_swap{h % 2}", 16)

                @blk.vector
                def _(v):
                    ctx.wait(v, "dve", tok_cs)
                    for h in range(H):
                        ctx.wv(v, "dve", f"d_swap{h % 2}", b_sw[h % 2] + 32 * (h // 2 + 1))
                        if h >= 1:
                            ctx.wv(v, "dve", f"d_stg{(h - 1) % 2}",
                                   b_st[(h - 1) % 2] + 16 * ((h - 1) // 2 + 1))
                        elif mat_i == 1:
                            ctx.wv(v, "dve", "d_stg0", b_st[0])
                            ctx.wv(v, "dve", "d_stg1", b_st[1])
                        t1 = ctx.inc(v.tensor_mul(att_sw[:, :], att_sw[:, :], sin_sb[:, :]),
                                     "rope")
                        ctx.wait(v, "dve", t1)
                        t2 = ctx.inc(v.tensor_mul(att_st[:, :], att_st[:, :],
                                                  cos_sb[:, :]), "rope")
                        ctx.wait(v, "dve", t2)
                        t3 = ctx.inc(v.tensor_add(att_bf[:, :], att_st[:, :], att_sw[:, :]),
                                     "rope")


            ctx.barrier()

            b_vl = ctx.n("d_vl")
            b_vpe = ctx.n("v_pe")
            b_vev = ctx.n("v_ev")
            b_vst = [ctx.n("d_vst0"), ctx.n("d_vst1")]

            @blk.gpsimd
            def _(g):
                for et in range(4):
                    if et >= 1:
                        ctx.wv(g, "gps", "v_pe", b_vpe + 8 * et)
                    src = w_attn["wv"][:, et * 512:(et + 1) * 512]
                    ctx.inc(g.dma_start(
                        out=vslab[:, :].rearrange("p (ic e) -> p ic e", e=512),
                        in_=src.rearrange("(ic p) e -> p ic e", p=128)), "d_vl", 16)

            @blk.tensor
            def _(t):
                ctx.wait(t, "pe", ctx.tok("qk_ev"))
                for et in range(4):
                    ctx.wv(t, "pe", "d_vl", b_vl + 16 * (et + 1))
                    for tc in range(8):
                        u = et * 8 + tc
                        if u >= 2:
                            t.wait_ge(ctx.sem("v_ev"), b_vev + u - 1)
                        for ic in range(EC):
                            mm = t.matmul(ps[u % 2][:, 0:TT],
                                          xdq[:, ic * TOK + tc * 128:ic * TOK + (tc + 1) * 128],
                                          vslab[:, ic * 512:(ic + 1) * 512],
                                          start=(ic == 0), stop=(ic == EC - 1))
                        ctx.inc(mm, "v_pe")

            @blk.vector
            def _(v):
                ctx.wait(v, "dve", tok_bv)
                for et in range(4):
                    for tc in range(8):
                        u = et * 8 + tc
                        ctx.wv(v, "dve", "v_pe", b_vpe + u + 1)
                        if u >= 1:
                            ctx.wv(v, "dve", f"d_vst{(u - 1) % 2}",
                                   b_vst[(u - 1) % 2] + 64 * ((u - 1) // 2 + 1))
                        e = v.tensor_add(vevac[:, :], ps[u % 2][:, 0:TT],
                                         bvb[:, et * 512:(et + 1) * 512])
                        ctx.inc(e, "v_ev")

            @blk.sync
            def _(sp):
                for et in range(4):
                    for tc in range(8):
                        u = et * 8 + tc
                        ctx.wv(sp, "sp", "v_ev", b_vev + u + 1)
                        for hq in range(4):
                            h = et * 4 + hq
                            ctx.inc(sp.dma_start(
                                out=vag_in[h][tc * 128:(tc + 1) * 128, :],
                                in_=vevac[:, hq * 128:(hq + 1) * 128]), f"d_vst{u % 2}", 16)
            ctx.barrier()

            qsc.close()
            flsc = Scope()
            kh = flsc.sbuf("kh", [128, S], BF16)
            qh = flsc.sbuf("qh", [128, S], BF16)
            vexth = flsc.sbuf("vexth", [128, 64 * 129], BF16)
            attn_sb = [flsc.sbuf(f"attn_sb{i}", [128, TT], BF16) for i in range(2)]
            otok = flsc.sbuf("otok", [128, 128], F32)
            obuf = flsc.sbuf("obuf", [128, TT], BF16)

            qkv_tail = [ctx.tok(n) for n in
                        ("d_stg0", "d_stg1", "d_vst0", "d_vst1", "d_swap0",
                         "d_swap1", "d_qk0", "d_qk1", "d_vl", "d_bv",
                         "v_ev", "rope", "bc_dv")]

            @blk.vector
            def _(v):
                for tk in qkv_tail:
                    ctx.wait(v, "dve", tk)
                ctx.inc(v.memset(vexth[:], 1.0), "s_init")
            tok_vx = ctx.tok("s_init")

            @blk.gpsimd
            def _(g):
                for tk in qkv_tail:
                    ctx.wait(g, "gps", tk)
                for p in range(2):
                    for (src, dst, nm) in ((kag_in, kag_out, "k"), (qag_in, qag_out, "q"),
                                           (vag_in, vag_out, "v")):
                        cc = g.collective_compute(
                            "AllGather", ALU.bypass, replica_groups=[list(range(NCORE))],
                            ins=[src[8 * p:8 * (p + 1)].rearrange("h a b -> (h a b)")[:]],
                            outs=[dst[p].rearrange("r h a b -> (r h a b)")[:]])
                        ctx.inc(cc, f"ag_kqv{p}")
            tok_agkqv = [ctx.tok("ag_kqv0"), ctx.tok("ag_kqv1")]

            for hi in range(HPC):
                b_kl = ctx.n("d_kh")
                b_sc = ctx.n("fl_sc")
                b_ex = ctx.n("fl_ex")
                b_mk = ctx.n("fl_mk")
                b_av = ctx.n("fl_av")
                b_fin = ctx.n("fl_fin")
                b_tr = ctx.n("fl_tr")
                b_oe = ctx.n("fl_oe")
                b_ost = [ctx.n("d_ost0"), ctx.n("d_ost1")]
                b_msq = ctx.n("m_sq")
                b_mmm = ctx.n("m_mm")
                b_mrd = ctx.n("m_rd")

                @blk.gpsimd
                def _(g):
                    ctx.wait(g, "gps", tok_vx)
                    for tk in tok_agkqv:
                        ctx.wait(g, "gps", tk)
                    for arm in g.switch_core_id(128):
                        my = arm.logical % 8
                        h = my * HPC + hi
                        p, hs = h // 8, h % 8
                        for r in range(NCORE):
                            g.dma_start(out=kh[:, r * TOK:(r + 1) * TOK],
                                        in_=kag_out[p, r, hs]).then_inc(ctx.sem("d_kh"), 16)
                            g.dma_start(out=qh[:, r * TOK:(r + 1) * TOK],
                                        in_=qag_out[p, r, hs]).then_inc(ctx.sem("d_kh"), 16)
                            g.dma_start(
                                out=vexth[:, :].rearrange("p (c n) -> p c n", n=129)[
                                    :, r * 8:(r + 1) * 8, 0:128],
                                in_=vag_out[p, r, hs].rearrange("(c p) d -> p c d", p=128)
                            ).then_inc(ctx.sem("d_kh"), 16)
                    ctx.counts[ctx._r("d_kh")] += 16 * 24
                tok_kh = ctx.tok("d_kh")

                @blk.scalar
                def _(s):
                    ctx.wait(s, "act", tok_kh)
                    a = s.activation(out=xdq[:, 0:S], in_=qh[:, :], func=AF.Square)
                    ctx.inc(a, "m_sq")
                    a = s.activation(out=xdq[:, S:2 * S], in_=kh[:, :], func=AF.Square)
                    ctx.inc(a, "m_sq")
                tok_msq = ctx.tok("m_sq")

                @blk.tensor
                def _(t):
                    ctx.wait(t, "pe", tok_msq)
                    for z in range(32):
                        if z >= 2:
                            t.wait_ge(ctx.sem("m_rd"), b_mrd + z - 1)
                        mm = t.matmul(ps[6 + (z % 2)][0:1, 0:512], ones128b[:, 0:1],
                                      xdq[:, z * 512:(z + 1) * 512], start=True, stop=True)
                        ctx.inc(mm, "m_mm")

                @blk.vector
                def _(v):
                    for z in range(32):
                        ctx.wv(v, "dve", "m_mm", b_mmm + z + 1)
                        r = v.reduce_max(rowE[0:1, 32 + z:33 + z],
                                         ps[6 + (z % 2)][0:1, 0:512], axis=AXX)
                        ctx.inc(r, "m_rd")
                    ctx.wait(v, "dve", ctx.tok("m_rd"))
                    r = v.reduce_max(rowE[0:1, 20:21], rowE[0:1, 32:48], axis=AXX)
                    t1 = ctx.inc(r, "m_rd")
                    r = v.reduce_max(rowE[0:1, 21:22], rowE[0:1, 48:64], axis=AXX)
                    t2 = ctx.inc(r, "m_rd")
                    ctx.wait(v, "dve", t2)
                    t3 = ctx.inc(v.tensor_mul(rowE[0:1, 20:21], rowE[0:1, 20:21],
                                              rowE[0:1, 21:22]), "m_rd")
                tok_mrd = ctx.tok("m_rd")

                @blk.scalar
                def _(s):
                    ctx.wait(s, "act", tok_mrd)
                    a = s.activation(out=rowE[0:1, 20:21], in_=rowE[0:1, 20:21],
                                     func=AF.Sqrt, scale=1.0 / 128.0)
                    ctx.inc(a, "m_rd2")
                tok_mrd2 = ctx.tok("m_rd2")

                @blk.vector
                def _(v):
                    ctx.wait(v, "dve", tok_mrd2)
                    ctx.inc(v.tensor_scalar(out=rowE[0:1, 20:21], in0=rowE[0:1, 20:21],
                                            scalar1=1.0, scalar2=-1.0,
                                            op0=ALU.add, op1=ALU.mult), "m_rd3")
                tok_mrd3 = ctx.tok("m_rd3")

                @blk.tensor
                def _(t):
                    ctx.wait(t, "pe", tok_mrd3)
                    mm = t.matmul(ps[6][:, 0:1], ones1f[0:1, 0:128], rowE[0:1, 20:21],
                                  start=True, stop=True)
                    ctx.inc(mm, "m_bc")
                tok_mbc = ctx.tok("m_bc")

                @blk.vector
                def _(v):
                    ctx.wait(v, "dve", tok_mbc)
                    ctx.inc(v.tensor_copy(mneg[:, hi:hi + 1], ps[6][:, 0:1]), "m_ok")
                tok_mok = ctx.tok("m_ok")

                units = []
                for u in range(16):
                    for kb in range(u + 1):
                        for kkc in range(4):
                            units.append((u, kb, kkc))
                n_units = len(units)

                @blk.tensor
                def _(t):
                    ctx.wait(t, "pe", tok_mok)
                    ui = 0
                    for u in range(16):
                        if u >= 1:
                            t.wait_ge(ctx.sem("fl_fin"), b_fin + 8 * u)
                        for kb in range(u + 1):
                            for kkc in range(4):
                                ui += 1
                                bsc = ps[ui % 2]
                                if ui > 2:
                                    t.wait_ge(ctx.sem("fl_ex"), b_ex + ui - 2)
                                kcol = (kb * 4 + kkc) * 128
                                mm = t.matmul(bsc[:, 0:512], kh[:, kcol:kcol + 128],
                                              qh[:, u * 512:(u + 1) * 512],
                                              start=True, stop=True)
                                ctx.inc(mm, "fl_sc")
                                diag = (kb == u)
                                need = ("fl_mk", b_mk + (u * 4 + kkc + 1)) if diag else \
                                       ("fl_ex", b_ex + ui)
                                t.wait_ge(ctx.sem(need[0]), need[1])
                                q4s = range(kkc, 4) if diag else range(4)
                                mm2 = None
                                for q4 in q4s:
                                    st = (kb == 0 and kkc == 0)
                                    sp_ = diag and (kkc == q4)
                                    mm2 = t.matmul(ps[2 + q4][:, 0:129],
                                                   attn_sb[ui % 2][:, q4 * 128:(q4 + 1) * 128],
                                                   vexth[:, (kb * 4 + kkc) * 129:
                                                         (kb * 4 + kkc + 1) * 129],
                                                   start=st, stop=sp_)
                                ctx.inc(mm2, "fl_av")
                        for q4 in range(4):
                            t.wait_ge(ctx.sem("fl_fin"), b_fin + 8 * u + 2 * (q4 + 1))
                            if u * 4 + q4 >= 1:
                                t.wait_ge(ctx.sem("fl_oe"), b_oe + u * 4 + q4)
                            mm = t.transpose(ps[6][:, 0:128], otok[:, :], identf[:, :])
                            ctx.inc(mm, "fl_tr")

                @blk.scalar
                def _(s):
                    ui = 0
                    for u in range(16):
                        for kb in range(u + 1):
                            for kkc in range(4):
                                ui += 1
                                bsc = ps[ui % 2]
                                ctx.wv(s, "act", "fl_sc", b_sc + ui)
                                if ui > 2:
                                    s.wait_ge(ctx.sem("fl_av"), b_av + ui - 2)
                                a = s.activation(out=attn_sb[ui % 2][:, :], in_=bsc[:, 0:512],
                                                 func=AF.Exp, scale=INV_SQRT_D,
                                                 bias=mneg[:, hi:hi + 1])
                                ctx.inc(a, "fl_ex")

                @blk.vector
                def _(v):
                    ui = 0
                    for u in range(16):
                        for kb in range(u + 1):
                            for kkc in range(4):
                                ui += 1
                                if kb == u:
                                    ctx.wv(v, "dve", "fl_ex", b_ex + ui)
                                    mk = v.tensor_mul(
                                        attn_sb[ui % 2][:, kkc * 128:(kkc + 1) * 128],
                                        attn_sb[ui % 2][:, kkc * 128:(kkc + 1) * 128],
                                        tri[:, :])
                                    ctx.inc(mk, "fl_mk")
                        ctx.wv(v, "dve", "fl_av", b_av + 2 * (u + 1) * (u + 2))
                        if u >= 1:
                            ctx.wv(v, "dve", f"d_ost{(u - 1) % 2}",
                                   b_ost[(u - 1) % 2] + 16 * ((u - 1) // 2 + 1))
                        else:
                            ctx.wv(v, "dve", "d_ost0", b_ost[0])
                            ctx.wv(v, "dve", "d_ost1", b_ost[1])
                        for q4 in range(4):
                            r = v.reciprocal(zr_col[:, q4:q4 + 1], ps[2 + q4][:, 128:129])
                            t1 = ctx.inc(r, "fl_fin")
                            ctx.wait(v, "dve", t1)
                            o = v.tensor_scalar(out=otok[:, :], in0=ps[2 + q4][:, 0:128],
                                                scalar1=zr_col[:, q4:q4 + 1], scalar2=None,
                                                op0=ALU.mult)
                            t2 = ctx.inc(o, "fl_fin")
                            ctx.wv(v, "dve", "fl_tr", b_tr + u * 4 + q4 + 1)
                            cp = v.tensor_copy(obuf[:, q4 * 128:(q4 + 1) * 128],
                                               ps[6][:, 0:128])
                            ctx.inc(cp, "fl_oe")

                @blk.sync
                def _(sp):
                    for u in range(16):
                        ctx.wv(sp, "sp", "fl_oe", b_oe + 4 * (u + 1))
                        ctx.inc(sp.dma_start(out=oag_in[hi][:, u * 512:(u + 1) * 512],
                                             in_=obuf[:, :]), f"d_ost{u % 2}", 16)
                ctx.barrier()

            @blk.gpsimd
            def _(g):
                ctx.wait(g, "gps", ctx.tok("d_ost0"))
                ctx.wait(g, "gps", ctx.tok("d_ost1"))
                cc = g.collective_compute(
                    "AllGather", ALU.bypass, replica_groups=[list(range(NCORE))],
                    ins=[oag_in.rearrange("h a b -> (h a b)")[:]],
                    outs=[oag_out.rearrange("r h a b -> (r h a b)")[:]])
                ctx.inc(cc, "ag_o")
            tok_ago = ctx.tok("ag_o")

            flsc.close()
            osc = Scope()
            ofm_all = osc.sbuf("ofm_all", [128, EC * TOK], BF16)
            qkslab = [osc.sbuf(f"oslab{i}", [128, EC * 128], BF16) for i in range(2)]

            @blk.gpsimd
            def _(g):
                ctx.wait(g, "gps", tok_ago)
                for arm in g.switch_core_id(128):
                    my = arm.logical % 8
                    for hh in range(H):
                        r, s_ = hh // 2, hh % 2
                        g.dma_start(
                            out=ofm_all[:, hh * TOK:(hh + 1) * TOK],
                            in_=oag_out[r, s_][:, my * TOK:(my + 1) * TOK]
                        ).then_inc(ctx.sem("d_ofm"), 16)
                ctx.counts[ctx._r("d_ofm")] += 16 * H
            tok_ofm = ctx.tok("d_ofm")

            b_ol = [ctx.n("d_wo0"), ctx.n("d_wo1")]
            b_ope = ctx.n("o_pe")
            b_oep = ctx.n("o_ep")

            @blk.gpsimd
            def _(g):
                for eo in range(EC):
                    if eo >= 2:
                        ctx.wv(g, "gps", "o_pe", b_ope + 2 * (eo - 1))
                    src = w_attn["wo"][:, eo * 128:(eo + 1) * 128]
                    ctx.inc(g.dma_start(
                        out=qkslab[eo % 2][:, :].rearrange("p (ic e) -> p ic e", e=128),
                        in_=src.rearrange("(ic p) e -> p ic e", p=128)),
                        f"d_wo{eo % 2}", 16)

            @blk.tensor
            def _(t):
                ctx.wait(t, "pe", tok_ofm)
                for eo in range(EC):
                    ctx.wv(t, "pe", f"d_wo{eo % 2}", b_ol[eo % 2] + 16 * (eo // 2 + 1))
                    for tt2 in range(2):
                        u = eo * 2 + tt2
                        if u >= 2:
                            t.wait_ge(ctx.sem("o_ep"), b_oep + u - 1)
                        for ic in range(EC):
                            mm = t.matmul(ps[u % 2][:, 0:TT],
                                          qkslab[eo % 2][:, ic * 128:(ic + 1) * 128],
                                          ofm_all[:, ic * TOK + tt2 * TT:
                                                  ic * TOK + tt2 * TT + TT],
                                          start=(ic == 0), stop=(ic == EC - 1))
                        ctx.inc(mm, "o_pe")

            @blk.vector
            def _(v):
                for eo in range(EC):
                    for tt2 in range(2):
                        u = eo * 2 + tt2
                        ctx.wv(v, "dve", "o_pe", b_ope + u + 1)
                        e = v.scalar_tensor_tensor(
                            out=x[eo][:, tt2 * TT:(tt2 + 1) * TT], in0=ps[u % 2][:, 0:TT],
                            scalar=bias_sb[:, 3 * EC + eo:3 * EC + eo + 1],
                            in1=x[eo][:, tt2 * TT:(tt2 + 1) * TT],
                            op0=ALU.add, op1=ALU.add)
                        ctx.inc(e, "o_ep")
            ctx.barrier()
            osc.close()

        if n_phases >= PH_FFN1 and FFN_STOP != "none":
            ffn_pass(1)
        if n_phases >= PH_ATTN:
            attn_phase()
        if n_phases >= PH_FFN2:
            ffn_pass(2)

        @blk.sync
        def _(sp):
            ctx.wait(sp, "sp", ctx.tok("dep"))
            ctx.wait(sp, "sp", ctx.tok("o_ep"))
            for i in range(EC):
                ctx.inc(sp.dma_start(out=y_out[128 * i:128 * (i + 1), :], in_=x[i][:]),
                        "d_y", 16)
            sp.wait_ge(ctx.sem("d_y"), ctx.counts["d_y"])

    return nc



_PERM = np.concatenate([np.arange(0, D, 2), np.arange(1, D, 2)])


def _rope_tables():
    pos = np.arange(S, dtype=np.float64)[:, None]
    exps = -np.arange(0, D, 2, dtype=np.float64) / D
    thetas = np.repeat(np.power(10000.0, exps), 2)
    ang = (pos * thetas[None, :])
    cos = np.cos(ang).astype(np.float32)
    sin = np.sin(ang).astype(np.float32)
    posf = np.arange(S, dtype=np.float32)[:, None]
    expsf = -np.arange(0, D, 2, dtype=np.float32) / np.float32(D)
    thetasf = np.repeat(np.power(np.float32(10000.0), expsf), 2)
    angf = posf * thetasf[None, :]
    return np.cos(angf).astype(np.float32), np.sin(angf).astype(np.float32)


def kernel(L0_attn_x, L0_attn_Wq, L0_attn_bq, L0_attn_Wk, L0_attn_bk,
           L0_attn_Wv, L0_attn_bv, L0_attn_Wo, L0_attn_bo,
           L0_ffn_w_gate, L0_ffn_w_up, L0_ffn_w_down):
    x = np.ascontiguousarray(np.asarray(L0_attn_x, dtype=np.float32).reshape(S, E))
    cos, sin = _rope_tables()

    perm_full = np.concatenate([h * D + _PERM for h in range(H)])
    Wq = np.ascontiguousarray(np.asarray(L0_attn_Wq, np.float32)[:, perm_full])
    Wk = np.ascontiguousarray(np.asarray(L0_attn_Wk, np.float32)[:, perm_full])
    bq = np.ascontiguousarray(np.asarray(L0_attn_bq, np.float32)[perm_full])[:, None]
    bk = np.ascontiguousarray(np.asarray(L0_attn_bk, np.float32)[perm_full])[:, None]
    Wv = np.ascontiguousarray(np.asarray(L0_attn_Wv, np.float32))
    bv = np.ascontiguousarray(np.asarray(L0_attn_bv, np.float32))[:, None]
    Wo = np.ascontiguousarray(np.asarray(L0_attn_Wo, np.float32))
    bo = np.ascontiguousarray(np.asarray(L0_attn_bo, np.float32))[:, None]

    wgT_f = np.ascontiguousarray(np.asarray(L0_ffn_w_gate, np.float32).T)
    wuT_f = np.ascontiguousarray(np.asarray(L0_ffn_w_up, np.float32).T)
    wdT_f = np.ascontiguousarray(np.asarray(L0_ffn_w_down, np.float32).T)

    cosp = cos[:, _PERM]
    sinp = sin[:, _PERM]
    sgn = np.ones(D, dtype=np.float32)
    sgn[: D // 2] = -1.0
    sins = sinp * sgn[None, :]

    tri_np = np.triu(np.ones((128, 128), np.float32)).astype(ml_dtypes.bfloat16)
    ident_np = np.eye(128, dtype=np.float32).astype(ml_dtypes.bfloat16)

    nc = build_program(N_PHASES)
    in_maps = []
    for c in range(NCORE):
        tsl = slice(c * TOK, (c + 1) * TOK)
        in_maps.append({
            "x_in": np.ascontiguousarray(x[tsl].T),
            "wq": Wq, "wk": Wk, "wv": Wv, "wo": Wo,
            "b_wq": bq, "b_wk": bk, "b_wv": bv, "b_wo": bo,
            "wgT": np.ascontiguousarray(wgT_f[:, tsl]),
            "wuT": np.ascontiguousarray(wuT_f[:, tsl]),
            "wdT": np.ascontiguousarray(wdT_f[tsl, :]),
            "cosr": np.ascontiguousarray(cosp[tsl].T),
            "sinr": np.ascontiguousarray(sins[tsl].T),
            "tri": tri_np,
            "ident": ident_np,
        })

    res = run_bass_kernel_spmd(nc, in_maps, core_ids=list(range(NCORE)))
    out = np.empty((S, E), dtype=np.float32)
    for c in range(NCORE):
        out[c * TOK:(c + 1) * TOK] = res.results[c]["y_out"].T
    return out.reshape(1, S, E)


# version 11 (final)
# speedup vs baseline: 1.0331x; 1.0331x over previous
import math
import numpy as np
import ml_dtypes
from contextlib import ExitStack

import concourse.bass as bass
import concourse.mybir as mybir
from concourse.bass_utils import run_bass_kernel_spmd

F32 = mybir.dt.float32
BF16 = mybir.dt.bfloat16
I32 = mybir.dt.int32
AF = mybir.ActivationFunctionType
ALU = mybir.AluOpType
AXX = mybir.AxisListType.X

E, EC = 2048, 16
J, JC = 8192, 64
H, D = 16, 128
TOK, TT = 1024, 512
S = 8192
NCORE = 8
JG = 128
NJG = J // JG
HPC = 2

LN_EPS = 1e-5
Q_EPS = 1e-5
INV_SQRT_D = 1.0 / math.sqrt(128.0)
INV_E = 1.0 / E
INV_127 = 1.0 / 127.0

PH_FFN1 = 1
PH_ATTN = 2
PH_FFN2 = 3
N_PHASES = 3
FFN_STOP = "full"


class Ctx:
    def __init__(self, nc, stack):
        self.nc = nc
        self.stack = stack
        self.sems = {}
        self.counts = {}
        self.watermark = {}

    ALIAS = {
        "d_qk0": "dqs0", "d_qk1": "dqs1", "d_wo0": "dqs0", "d_wo1": "dqs1",
        "d_vl": "g_d", "d_kh": "d_q2s0", "d_ofm": "d_q2s1",
        "d_swap0": "dguu0", "d_swap1": "dguu1",
        "d_stg0": "d_hrd0", "d_stg1": "d_hrd1",
        "d_vst0": "dws_g0", "d_vst1": "dws_g1",
        "d_ost0": "dws_u0", "d_ost1": "dws_u1",
        "d_bv": "dws_d0",
        "qk_pe": "gu_g", "qk_ev": "sil", "rope": "h_w",
        "v_pe": "gu_u", "v_ev": "h_ok",
        "m_sq": "ln_sa", "m_mm": "ln_mm", "m_rd": "ln_mr", "m_rd2": "ln_rs",
        "m_rd3": "ln_rc", "m_bc": "ln_tp", "m_ok": "ln_xr",
        "fl_sc": "q2", "fl_ex": "hrow", "fl_mk": "hred", "fl_av": "dpe",
        "fl_fin": "dep", "fl_tr": "htr", "fl_oe": "htr2",
        "o_pe": "ln_rw", "o_ep": "ln_app",
        "ag_kqv0": "ag_g", "ag_kqv1": "ag_u", "ag_o": "ag_d",
    }

    def _r(self, name):
        return self.ALIAS.get(name, name)

    def sem(self, name):
        name = self._r(name)
        if name not in self.sems:
            self.sems[name] = self.stack.enter_context(self.nc.semaphore(name))
            self.counts[name] = 0
        return self.sems[name]

    def n(self, name):
        self.sem(name)
        return self.counts[self._r(name)]

    def inc(self, instr, name, amount=1):
        name = self._r(name)
        s = self.sem(name)
        instr.then_inc(s, amount)
        self.counts[name] += amount
        return (name, self.counts[name])

    def tok(self, name):
        name = self._r(name)
        self.sem(name)
        return (name, self.counts[name])

    def wait(self, eng, ename, token):
        name, val = token
        name = self._r(name)
        if val <= 0:
            return
        key = (ename, name)
        if self.watermark.get(key, -1) >= val:
            return
        eng.wait_ge(self.sem(name), val)
        self.watermark[key] = val

    def wv(self, eng, ename, name, val):
        self.wait(eng, ename, (name, val))

    _ASYNC_PFX = ("d_", "dws", "dgu", "dqs", "g_d", "cc", "ag_")

    def barrier(self):
        self.bar_k = getattr(self, "bar_k", 0) + 1
        tgt = 5 * self.bar_k
        bar = self.sem("bar")
        blk = self._blk
        for attr in ("sync", "tensor", "scalar", "vector", "gpsimd"):
            def body(e):
                e.sem_inc(bar, 1)
                e.wait_ge(bar, tgt)
            getattr(blk, attr)(body)


def build_program(n_phases=N_PHASES):
    nc = bass.Bass("TRN2", target_bir_lowering=False)

    x_in = nc.dram_tensor("x_in", [E, TOK], F32, kind="ExternalInput").ap()
    y_out = nc.dram_tensor("y_out", [E, TOK], F32, kind="ExternalOutput").ap()
    w_attn = {nm: nc.dram_tensor(nm, [E, E], F32, kind="ExternalInput").ap()
              for nm in ("wq", "wk", "wv", "wo")}
    b_attn = {nm: nc.dram_tensor("b_" + nm, [E, 1], F32, kind="ExternalInput").ap()
              for nm in ("wq", "wk", "wv", "wo")}
    wgT = nc.dram_tensor("wgT", [E, TOK], F32, kind="ExternalInput").ap()
    wuT = nc.dram_tensor("wuT", [E, TOK], F32, kind="ExternalInput").ap()
    wdT = nc.dram_tensor("wdT", [TOK, E], F32, kind="ExternalInput").ap()
    cosr = nc.dram_tensor("cosr", [D, TOK], F32, kind="ExternalInput").ap()
    sinr = nc.dram_tensor("sinr", [D, TOK], F32, kind="ExternalInput").ap()
    tri_in = nc.dram_tensor("tri", [128, 128], BF16, kind="ExternalInput").ap()
    ident_in = nc.dram_tensor("ident", [128, 128], BF16, kind="ExternalInput").ap()

    def idram(name, shape, dt):
        return nc.dram_tensor(name, shape, dt, kind="Internal", addr_space="Local").ap()

    h_dram = idram("h_dram", [2, JC, 128, TT], F32)
    q2_dram = idram("q2_dram", [2, JC, 128, TT], BF16)
    wsq_in = idram("wsq_in", [8], F32)
    wsq_out = idram("wsq_out", [NCORE, 8], F32)
    wq_in_d = {"g": idram("wqg_in", [E, TOK], BF16),
               "u": idram("wqu_in", [E, TOK], BF16),
               "d": idram("wqd_in", [TOK, E], BF16)}
    wq_out_d = {"g": idram("wqg_g", [NCORE, E, TOK], BF16),
                "u": idram("wqu_g", [NCORE, E, TOK], BF16),
                "d": idram("wqd_g", [NCORE, TOK, E], BF16)}
    kag_in = idram("kag_in", [H, D, TOK], BF16)
    qag_in = idram("qag_in", [H, D, TOK], BF16)
    vag_in = idram("vag_in", [H, TOK, D], BF16)
    kag_out = idram("kag_out", [2, NCORE, 8, D, TOK], BF16)
    qag_out = idram("qag_out", [2, NCORE, 8, D, TOK], BF16)
    vag_out = idram("vag_out", [2, NCORE, 8, TOK, D], BF16)
    oag_in = idram("oag_in", [HPC, D, S], BF16)
    oag_out = idram("oag_out", [HPC, NCORE, D, S], BF16)

    stack = ExitStack()
    ctx = Ctx(nc, stack)

    def sbuf(name, shape, dt):
        return stack.enter_context(nc.sbuf_tensor(name, shape, dt))

    with stack:
        x = [sbuf(f"x{i}", [128, TOK], F32) for i in range(EC)]
        xdq = sbuf("xdq", [128, EC * TOK], BF16)
        ones1f = sbuf("ones1f", [1, 128], F32)
        ones128b = sbuf("ones128b", [128, 1], BF16)
        ones128f = sbuf("ones128f", [128, 1], F32)
        tri = sbuf("tri_sb", [128, 128], BF16)
        ident = sbuf("ident_sb", [128, 128], BF16)
        identf = sbuf("identf", [128, 128], F32)
        bias_sb = sbuf("bias_sb", [128, 4 * EC], F32)
        wsb = sbuf("wsb", [128, 12], F32)
        rowA = sbuf("rowA", [1, TOK], F32)
        rowB = sbuf("rowB", [1, TOK], F32)
        rowC = sbuf("rowC", [1, TOK], F32)
        rowD = sbuf("rowD", [1, TOK], F32)
        rowT = sbuf("rowT", [1, TOK], F32)
        rowE = sbuf("rowE", [1, 128], F32)
        colA = sbuf("colA", [128, 16], F32)
        accA = sbuf("accA", [128, TOK], F32)
        accB = sbuf("accB", [128, TOK], F32)
        bcA = sbuf("bcA", [128, TOK], F32)
        bcB = sbuf("bcB", [128, TOK], F32)
        qscr = {}
        wqbf = [sbuf(f"wqbf{i}", [128, E], BF16) for i in range(2)]
        zr_col = sbuf("zr_col", [128, 4], F32)
        mneg = sbuf("mneg", [128, 2], F32)

        ps = [stack.enter_context(nc.psum_tensor(f"ps{i}", [128, 512], F32))
              for i in range(8)]

        blk = stack.enter_context(nc.Block())
        ctx._blk = blk

        class Scope:
            def __init__(self):
                self.st = ExitStack()

            def sbuf(self, name, shape, dt):
                return self.st.enter_context(nc.sbuf_tensor(name, shape, dt))

            def close(self):
                self.st.close()

        def emit_bcast(quants, row_tok):
            b_pe = ctx.n("bc_pe")
            b_dv = ctx.n("bc_dv")

            @blk.tensor
            def _(t):
                ctx.wait(t, "pe", row_tok)
                k = 0
                for row, bc in quants:
                    for hh in range(2):
                        k += 1
                        if k > 2:
                            t.wait_ge(ctx.sem("bc_dv"), b_dv + k - 2)
                        bank = ps[6 + (k % 2)]
                        mm = t.matmul(bank[:, 0:512], ones1f[0:1, 0:128],
                                      row[0:1, hh * 512:(hh + 1) * 512],
                                      start=True, stop=True)
                        ctx.inc(mm, "bc_pe")

            @blk.vector
            def _(v):
                k = 0
                for row, bc in quants:
                    for hh in range(2):
                        k += 1
                        ctx.wv(v, "dve", "bc_pe", b_pe + k)
                        bank = ps[6 + (k % 2)]
                        cp = v.tensor_copy(bc[:, hh * 512:(hh + 1) * 512], bank[:, 0:512])
                        ctx.inc(cp, "bc_dv")
            return ctx.tok("bc_dv")

        def emit_ln(src, out_bf, do_quant, src_toks):
            if not isinstance(src_toks, list):
                src_toks = [src_toks]
            """src: list of 16 [128,1024] f32; out to out_bf (xdq layout).
            Uses acc*/row*/bc* scratch. Returns apply-done token ("ln_app")."""
            if do_quant:
                accC, accD, accI, bcC = (qscr["accC"], qscr["accD"],
                                         qscr["accI"], qscr["bcC"])
            b_tr = ctx.n("ln_tr")
            b_sq = ctx.n("ln_sq")
            b_sa = ctx.n("ln_sa")
            b_mm = ctx.n("ln_mm")
            b_tp = ctx.n("ln_tp")
            b_mr = ctx.n("ln_mr")
            b_t2 = ctx.n("ln_t2")
            b_xr = ctx.n("ln_xr")
            b_rw = ctx.n("ln_rw")
            b_rc = ctx.n("ln_rc")
            b_rs = ctx.n("ln_rs")
            b_ap = ctx.n("ln_app")

            @blk.vector
            def _(v):
                for tk in src_toks:
                    ctx.wait(v, "dve", tk)
                ctx.inc(v.tensor_copy(accA[:, :], src[0][:, :]), "ln_tr")
                if do_quant:
                    ctx.inc(v.tensor_copy(accC[:, :], src[0][:, :]), "ln_tr")
                    ctx.inc(v.tensor_copy(accD[:, :], src[0][:, :]), "ln_tr")
                for c in range(1, EC):
                    ctx.wait(v, "dve", ctx.tok("ln_tr"))
                    ctx.inc(v.tensor_add(accA[:, :], accA[:, :], src[c][:, :]), "ln_tr")
                    if do_quant:
                        ctx.inc(v.tensor_max(accC[:, :], accC[:, :], src[c][:, :]), "ln_tr")
                        ctx.inc(v.tensor_tensor(accD[:, :], accD[:, :], src[c][:, :],
                                                op=ALU.min), "ln_tr")

            @blk.scalar
            def _(s):
                for tk in src_toks:
                    ctx.wait(s, "act", tk)
                for c in range(EC):
                    if c >= 1:
                        s.wait_ge(ctx.sem("ln_sq"), b_sq + c)
                    a = s.activation(out=bcA[:, :], in_=src[c][:, :], func=AF.Square)
                    ctx.inc(a, "ln_sa")

            @blk.vector
            def _(v):
                for c in range(EC):
                    ctx.wv(v, "dve", "ln_sa", b_sa + c + 1)
                    ctx.wait(v, "dve", ctx.tok("ln_sq"))
                    if c == 0:
                        i = v.tensor_copy(accB[:, :], bcA[:, :])
                    else:
                        i = v.tensor_add(accB[:, :], accB[:, :], bcA[:, :])
                    ctx.inc(i, "ln_sq")
            tok_tree = ctx.tok("ln_tr")
            tok_sq = ctx.tok("ln_sq")

            @blk.tensor
            def _(t):
                ctx.wait(t, "pe", tok_tree)
                ctx.wait(t, "pe", tok_sq)
                for hh in range(2):
                    mm = t.matmul(ps[0 + hh][0:1, 0:512], ones128f[:, 0:1],
                                  accA[:, hh * 512:(hh + 1) * 512], start=True, stop=True)
                    ctx.inc(mm, "ln_mm")
                    mm = t.matmul(ps[2 + hh][0:1, 0:512], ones128f[:, 0:1],
                                  accB[:, hh * 512:(hh + 1) * 512], start=True, stop=True)
                    ctx.inc(mm, "ln_mm")
                if do_quant:
                    for tb in range(8):
                        if tb >= 2:
                            t.wait_ge(ctx.sem("ln_mr"), b_mr + 2 * (tb - 1))
                        mm = t.transpose(ps[4 + tb % 2][:, 0:128],
                                         accC[:, tb * 128:(tb + 1) * 128], identf[:, :])
                        ctx.inc(mm, "ln_tp")
                        mm = t.transpose(ps[6 + tb % 2][:, 0:128],
                                         accD[:, tb * 128:(tb + 1) * 128], identf[:, :])
                        ctx.inc(mm, "ln_tp")

            @blk.vector
            def _(v):
                for hh in range(2):
                    ctx.wv(v, "dve", "ln_mm", b_mm + 2 * hh + 2)
                    ctx.inc(v.tensor_copy(rowA[0:1, hh * 512:(hh + 1) * 512],
                                          ps[0 + hh][0:1, 0:512]), "ln_rw")
                    ctx.inc(v.tensor_copy(rowB[0:1, hh * 512:(hh + 1) * 512],
                                          ps[2 + hh][0:1, 0:512]), "ln_rw")
                if do_quant:
                    for tb in range(8):
                        ctx.wv(v, "dve", "ln_tp", b_tp + 2 * tb + 2)
                        r = v.reduce_max(colA[:, tb:tb + 1], ps[4 + tb % 2][:, 0:128],
                                         axis=AXX)
                        ctx.inc(r, "ln_mr")
                        r = v.tensor_reduce(colA[:, 8 + tb:9 + tb],
                                            ps[6 + tb % 2][:, 0:128], axis=AXX, op=ALU.min)
                        ctx.inc(r, "ln_mr")
            tok_mr = ctx.tok("ln_mr")

            if do_quant:
                @blk.tensor
                def _(t):
                    ctx.wait(t, "pe", tok_mr)
                    for tb in range(8):
                        off = (tb % 4) * 128
                        if tb == 4:
                            t.wait_ge(ctx.sem("ln_xr"), b_xr + 2)
                        mm = t.transpose(ps[6][0:1, off:off + 128],
                                         colA[:, tb:tb + 1], identf[:, :])
                        ctx.inc(mm, "ln_t2")
                        mm = t.transpose(ps[7][0:1, off:off + 128],
                                         colA[:, 8 + tb:9 + tb], identf[:, :])
                        ctx.inc(mm, "ln_t2")

                @blk.vector
                def _(v):
                    for grp in range(2):
                        ctx.wv(v, "dve", "ln_t2", b_t2 + 8 * (grp + 1))
                        ctx.inc(v.tensor_copy(rowC[0:1, grp * 512:(grp + 1) * 512],
                                              ps[6][0:1, 0:512]), "ln_xr")
                        ctx.inc(v.tensor_copy(rowD[0:1, grp * 512:(grp + 1) * 512],
                                              ps[7][0:1, 0:512]), "ln_xr")

            @blk.vector
            def _(v):
                ctx.wait(v, "dve", ctx.tok("ln_rw"))
                if do_quant:
                    ctx.wait(v, "dve", ctx.tok("ln_xr"))

                def seq(i_):
                    ctx.wait(v, "dve", ctx.tok("ln_rc"))
                ctx.inc(v.tensor_scalar(out=rowA[0:1, :], in0=rowA[0:1, :],
                                        scalar1=INV_E, scalar2=None, op0=ALU.mult), "ln_rc")
                ctx.inc(v.tensor_scalar(out=rowB[0:1, :], in0=rowB[0:1, :],
                                        scalar1=INV_E, scalar2=None, op0=ALU.mult), "ln_rc")
                seq(0)
                ctx.inc(v.tensor_mul(rowT[0:1, :], rowA[0:1, :], rowA[0:1, :]), "ln_rc")
                seq(0)
                ctx.inc(v.tensor_sub(rowB[0:1, :], rowB[0:1, :], rowT[0:1, :]), "ln_rc")
                seq(0)
                ctx.inc(v.tensor_scalar(out=rowB[0:1, :], in0=rowB[0:1, :],
                                        scalar1=LN_EPS, scalar2=None, op0=ALU.add), "ln_rc")
                seq(0)
                ctx.inc(v.reciprocal(rowT[0:1, :], rowB[0:1, :]), "ln_rc")
            tok_rc1 = ctx.tok("ln_rc")

            @blk.scalar
            def _(s):
                ctx.wait(s, "act", tok_rc1)
                a = s.activation(out=rowT[0:1, :], in_=rowT[0:1, :], func=AF.Sqrt)
                ctx.inc(a, "ln_rs")
            tok_rs = ctx.tok("ln_rs")

            @blk.vector
            def _(v):
                ctx.wait(v, "dve", tok_rs)

                def seq(i_):
                    ctx.wait(v, "dve", ctx.tok("ln_rc"))
                if do_quant:
                    ctx.inc(v.tensor_sub(rowC[0:1, :], rowC[0:1, :], rowA[0:1, :]), "ln_rc")
                    ctx.inc(v.tensor_sub(rowD[0:1, :], rowA[0:1, :], rowD[0:1, :]), "ln_rc")
                    seq(0)
                    ctx.inc(v.tensor_max(rowC[0:1, :], rowC[0:1, :], rowD[0:1, :]), "ln_rc")
                    seq(0)
                    ctx.inc(v.tensor_mul(rowC[0:1, :], rowC[0:1, :], rowT[0:1, :]), "ln_rc")
                    seq(0)
                    ctx.inc(v.tensor_scalar(out=rowC[0:1, :], in0=rowC[0:1, :],
                                            scalar1=INV_127, scalar2=Q_EPS,
                                            op0=ALU.mult, op1=ALU.add), "ln_rc")
                    seq(0)
                    ctx.inc(v.reciprocal(rowD[0:1, :], rowC[0:1, :]), "ln_rc")
                    seq(0)
                    ctx.inc(v.tensor_mul(rowT[0:1, :], rowT[0:1, :], rowD[0:1, :]), "ln_rc")
                    seq(0)
                ctx.inc(v.tensor_mul(rowB[0:1, :], rowA[0:1, :], rowT[0:1, :]), "ln_rc")
            tok_rows = ctx.tok("ln_rc")

            quants = [(rowT, bcA), (rowB, bcB)] + ([(rowC, bcC)] if do_quant else [])
            tok_bc = emit_bcast(quants, tok_rows)

            @blk.vector
            def _(v):
                ctx.wait(v, "dve", tok_bc)

                def seq(i_):
                    ctx.wait(v, "dve", ctx.tok("ln_app"))
                for c in range(EC):
                    seq(0)
                    ctx.inc(v.tensor_mul(accA[:, :], src[c][:, :], bcA[:, :]), "ln_app")
                    seq(0)
                    if do_quant:
                        ctx.inc(v.tensor_sub(accB[:, :], accA[:, :], bcB[:, :]), "ln_app")
                        seq(0)
                        ctx.inc(v.tensor_scalar(out=accI[:, :], in0=accB[:, :],
                                                scalar1=1024.0, scalar2=None,
                                                op0=ALU.add), "ln_app")
                        seq(0)
                        ctx.inc(v.scalar_tensor_tensor(out=accC[:, :], in0=accI[:, :],
                                                       scalar=-1.0, in1=accB[:, :],
                                                       op0=ALU.mult, op1=ALU.add), "ln_app")
                        seq(0)
                        ctx.inc(v.tensor_scalar(out=accC[:, :], in0=accC[:, :],
                                                scalar1=-1023.5, scalar2=None,
                                                op0=ALU.is_ge), "ln_app")
                        seq(0)
                        ctx.inc(v.scalar_tensor_tensor(out=accD[:, :], in0=accI[:, :],
                                                       scalar=-1024.0, in1=accC[:, :],
                                                       op0=ALU.add, op1=ALU.add), "ln_app")
                        seq(0)
                        ctx.inc(v.tensor_mul(out_bf[:, c * TOK:(c + 1) * TOK],
                                             accD[:, :], bcC[:, :]), "ln_app")
                    else:
                        ctx.inc(v.tensor_sub(out_bf[:, c * TOK:(c + 1) * TOK],
                                             accA[:, :], bcB[:, :]), "ln_app")
            return ctx.tok("ln_app")

        @blk.sync
        def _(sp):
            for i in range(EC):
                ctx.inc(sp.dma_start(out=x[i][:], in_=x_in[128 * i:128 * (i + 1), :]),
                        "d_x", 16)
            ctx.inc(sp.dma_start(out=tri[:], in_=tri_in[:, :]), "d_c", 16)
            ctx.inc(sp.dma_start(out=ident[:], in_=ident_in[:, :]), "d_c", 16)
            for k, nm in enumerate(("wq", "wk", "wv", "wo")):
                ctx.inc(sp.dma_start(
                    out=bias_sb[:, k * EC:(k + 1) * EC],
                    in_=b_attn[nm].rearrange("(c p) one -> p (c one)", p=128),
                    allow_slow_non_contiguous=True), "d_c", 16)
        tok_x = ctx.tok("d_x")
        tok_c = ctx.tok("d_c")

        @blk.vector
        def _(v):
            ctx.inc(v.memset(ones1f[:], 1.0), "s_init")
            ctx.inc(v.memset(ones128b[:], 1.0), "s_init")
            ctx.inc(v.memset(ones128f[:], 1.0), "s_init")
            ctx.inc(v.memset(rowE[0:1, 0:128], 0.0), "s_init")
            ctx.wait(v, "dve", tok_c)
            ctx.inc(v.tensor_copy(identf[:, :], ident[:, :]), "s_init")
        tok_init = ctx.tok("s_init")

        wscope = Scope()
        wst = [wscope.sbuf(f"wst{i}", [128, E], F32) for i in range(2)]

        nslab = {"g": EC, "u": EC, "d": TOK // 128}
        wwid = {"g": TOK, "u": TOK, "d": E}
        wdrm = {"g": wgT, "u": wuT, "d": wdT}

        halves = {m: max(1, wwid[m] // TOK) for m in ("g", "u", "d")}

        @blk.sync
        def _(sp):
            prev = None
            for mat in ("g", "u", "d"):
                ctx.sem("wab_" + mat)
                if prev is not None:
                    ctx.wv(sp, "sp", "wab_" + prev, halves[prev] * nslab[prev])
                prev = mat
                for si in range(nslab[mat]):
                    if si >= 2:
                        ctx.wv(sp, "sp", "wab_" + mat, halves[mat] * (si - 1))
                    ctx.inc(sp.dma_start(out=wst[si % 2][:, 0:wwid[mat]],
                                         in_=wdrm[mat][128 * si:128 * (si + 1), :]),
                            f"dws_{mat}{si % 2}", 16)

        tA = wscope.sbuf("tA", [128, TOK], F32)
        tB = wscope.sbuf("tB", [128, TOK], F32)
        M23 = 8388608.0

        @blk.scalar
        def _(s):
            for mi, mat in enumerate(("g", "u", "d")):
                H_ = halves[mat]
                for si in range(nslab[mat]):
                    for hh in range(H_):
                        gidx = si * H_ + hh
                        ctx.wv(s, "act", f"dws_{mat}{si % 2}", 16 * (si // 2 + 1))
                        if gidx >= 1:
                            ctx.wv(s, "act", "ws_" + mat, gidx)
                        elif mi > 0:
                            pm = ("g", "u", "d")[mi - 1]
                            ctx.wv(s, "act", "ws_" + pm, halves[pm] * nslab[pm])
                        a = s.activation(out=tA[:, :],
                                         in_=wst[si % 2][:, hh * TOK:(hh + 1) * TOK],
                                         func=AF.Abs, scale=4096.0)
                        ctx.inc(a, "wab_" + mat)

        @blk.vector
        def _(v):
            def step(op):
                t = ctx.inc(op, "wch")
                ctx.wait(v, "dve", t)
            for mi, mat in enumerate(("g", "u", "d")):
                H_ = halves[mat]
                for si in range(nslab[mat]):
                    for hh in range(H_):
                        gidx = si * H_ + hh
                        ctx.wv(v, "dve", "wab_" + mat, gidx + 1)
                        step(v.tensor_scalar(out=tB[:, :], in0=tA[:, :],
                                             scalar1=M23, scalar2=None, op0=ALU.add))
                        step(v.tensor_scalar(out=tB[:, :], in0=tB[:, :],
                                             scalar1=-M23, scalar2=None, op0=ALU.add))
                        step(v.tensor_sub(tA[:, :], tA[:, :], tB[:, :]))
                        step(v.reduce_sum(colA[:, 6:7], tB[:, :], axis=AXX))
                        step(v.reduce_sum(colA[:, 7:8], tA[:, :], axis=AXX))
                        if gidx == 0:
                            step(v.tensor_copy(colA[:, mi:mi + 1], colA[:, 6:7]))
                            e = v.tensor_copy(colA[:, 3 + mi:4 + mi], colA[:, 7:8])
                        else:
                            step(v.tensor_add(colA[:, mi:mi + 1], colA[:, mi:mi + 1],
                                              colA[:, 6:7]))
                            e = v.tensor_add(colA[:, 3 + mi:4 + mi],
                                             colA[:, 3 + mi:4 + mi], colA[:, 7:8])
                        t = ctx.inc(e, "ws_" + mat)
                        ctx.wait(v, "dve", t)
            for mi in range(3):
                step(v.tensor_scalar(out=colA[:, 8 + mi:9 + mi], in0=colA[:, mi:mi + 1],
                                     scalar1=1.0 / 512.0, scalar2=None, op0=ALU.mult))
                step(v.tensor_scalar(out=colA[:, 6:7], in0=colA[:, 8 + mi:9 + mi],
                                     scalar1=M23, scalar2=None, op0=ALU.add))
                step(v.tensor_scalar(out=colA[:, 6:7], in0=colA[:, 6:7],
                                     scalar1=-M23, scalar2=None, op0=ALU.add))
                step(v.tensor_sub(colA[:, 11 + mi:12 + mi], colA[:, 8 + mi:9 + mi],
                                  colA[:, 6:7]))
                step(v.tensor_scalar(out=colA[:, 11 + mi:12 + mi],
                                     in0=colA[:, 11 + mi:12 + mi],
                                     scalar1=512.0, scalar2=None, op0=ALU.mult))
                e = v.tensor_copy(colA[:, 8 + mi:9 + mi], colA[:, 6:7])
                t = ctx.inc(e, "wsplit")
                ctx.wait(v, "dve", t)
        tok_ws = [ctx.tok("ws_" + m) for m in ("g", "u", "d")]
        tok_wsplit = ctx.tok("wsplit")

        @blk.tensor
        def _(t):
            ctx.wait(t, "pe", tok_init)
            ctx.wait(t, "pe", tok_wsplit)
            mm = t.matmul(ps[0][0:1, 0:11], ones128f[:, 0:1], colA[:, 3:14],
                          start=True, stop=True)
            ctx.inc(mm, "pe0")
        tok_wsmm = ctx.tok("pe0")

        @blk.vector
        def _(v):
            ctx.wait(v, "dve", tok_wsmm)
            v.tensor_copy(rowE[0:1, 0:3], ps[0][0:1, 5:8])
            t = ctx.inc(v.tensor_copy(rowE[0:1, 3:6], ps[0][0:1, 8:11]), "wch")
            ctx.wait(v, "dve", t)
            ctx.inc(v.tensor_add(rowE[0:1, 3:6], rowE[0:1, 3:6],
                                 ps[0][0:1, 0:3]), "wsrow")
        tok_wsrow = ctx.tok("wsrow")

        @blk.gpsimd
        def _(g):
            ctx.wait(g, "gps", tok_wsrow)
            ctx.wait(g, "gps", tok_init)
            ctx.inc(g.dma_start(out=wsq_in[0:8].rearrange("(one c) -> one c", one=1),
                                in_=rowE[0:1, 0:8]), "g_d", 16)
            ctx.wv(g, "gps", "g_d", ctx.n("g_d"))
            cc = g.collective_compute(
                "AllGather", ALU.bypass, replica_groups=[list(range(NCORE))],
                ins=[wsq_in[:]], outs=[wsq_out.rearrange("r c -> (r c)")[:]])
            tkcc = ctx.inc(cc, "cc")
            ctx.wait(g, "gps", tkcc)
            ctx.inc(g.dma_start(out=rowE[0:1, 16:16 + 64],
                                in_=wsq_out.rearrange("r c -> (r c)")[:]), "g_d", 16)
        tok_wsag = ctx.tok("g_d")

        @blk.vector
        def _(v):
            ctx.wait(v, "dve", tok_wsag)

            def seq(i_):
                ctx.wait(v, "dve", ctx.tok("wsc"))
            ctx.inc(v.reduce_sum(rowE[0:1, 0:8].rearrange("p (c r) -> p c r", r=8),
                                 rowE[0:1, 16:80].rearrange("p (r c) -> p c r", r=8),
                                 axis=AXX), "wsc")
            seq(0)
            ctx.inc(v.tensor_scalar(out=rowE[0:1, 11:14], in0=rowE[0:1, 3:6],
                                    scalar1=687194.75, scalar2=2.0 ** -36,
                                    op0=ALU.add, op1=ALU.mult), "wsc")
            ctx.inc(v.tensor_scalar(out=rowE[0:1, 8:11], in0=rowE[0:1, 0:3],
                                    scalar1=2.0 ** -27, scalar2=None,
                                    op0=ALU.mult), "wsc")
            seq(0)
            ctx.inc(v.tensor_add(rowE[0:1, 0:3], rowE[0:1, 8:11],
                                 rowE[0:1, 11:14]), "wsc")
            seq(0)
            ctx.inc(v.tensor_scalar(out=rowE[0:1, 3:4], in0=rowE[0:1, 2:3],
                                    scalar1=0.5, scalar2=None, op0=ALU.mult), "wsc")
            ctx.inc(v.tensor_scalar(out=rowE[0:1, 4:7], in0=rowE[0:1, 0:3],
                                    scalar1=0.4999998274, scalar2=None,
                                    op0=ALU.mult), "wsc")
            seq(0)
            ctx.inc(v.tensor_scalar(out=rowE[0:1, 7:10], in0=rowE[0:1, 4:7],
                                    scalar1=-1.0, scalar2=None, op0=ALU.mult), "wsc")
        tok_wsc = ctx.tok("wsc")

        @blk.tensor
        def _(t):
            ctx.wait(t, "pe", tok_wsc)
            mm = t.matmul(ps[1][:, 0:10], ones1f[0:1, 0:128], rowE[0:1, 0:10],
                          start=True, stop=True)
            ctx.inc(mm, "pe0")
        tok_wsbmm = ctx.tok("pe0")

        @blk.vector
        def _(v):
            ctx.wait(v, "dve", tok_wsbmm)
            ctx.inc(v.tensor_copy(wsb[:, 0:10], ps[1][:, 0:10]), "wsbok")
        tok_wsb = ctx.tok("wsbok")

        wq_reqs = []

        @blk.sync
        def _(sp):
            for tk in tok_ws:
                ctx.wait(sp, "sp", tk)
            prevq = None
            for mat in ("g", "u", "d"):
                b_q = ctx.n("wq_" + mat)
                if prevq is not None:
                    pper = 3 if wwid[prevq] <= TOK else 6
                    ctx.wv(sp, "sp", "wq_" + prevq, pper * nslab[prevq])
                prevq = mat
                for si in range(nslab[mat]):
                    if si >= 2:
                        ctx.wv(sp, "sp", "wq_" + mat, b_q + per * (si - 1) if False
                               else b_q + (3 if wwid[mat] <= TOK else 6) * (si - 1))
                    ctx.inc(sp.dma_start(out=wst[si % 2][:, 0:wwid[mat]],
                                         in_=wdrm[mat][128 * si:128 * (si + 1), :]),
                            f"dws_{mat}{si % 2}", 16)

        b_dqs = [ctx.n("dqs0"), ctx.n("dqs1")]

        @blk.vector
        def _(v):
            ctx.wait(v, "dve", tok_wsb)
            gidx = 0
            for mi, mat in enumerate(("g", "u", "d")):
                w = wwid[mat]
                for si in range(nslab[mat]):
                    buf = wst[si % 2]
                    ctx.wv(v, "dve", f"dws_{mat}{si % 2}",
                           16 * (nslab[mat] + si) // 32 * 16 if False else
                           16 * (nslab[mat] // 2 + (si // 2) + 1))
                    if gidx >= 2:
                        ctx.wv(v, "dve", f"dqs{gidx % 2}",
                               b_dqs[gidx % 2] + 16 * (gidx // 2))
                    for hh in range(max(1, w // TOK)):
                        sl = slice(hh * TOK, hh * TOK + min(TOK, w))
                        ctx.wait(v, "dve", ctx.tok("wq_" + mat))
                        t1 = ctx.inc(v.tensor_scalar(out=accA[:, 0:min(TOK, w)],
                                                     in0=buf[:, sl],
                                                     scalar1=wsb[:, 4 + mi:5 + mi],
                                                     scalar2=None, op0=ALU.is_gt),
                                     "wq_" + mat)
                        t2 = ctx.inc(v.tensor_scalar(out=accB[:, 0:min(TOK, w)],
                                                     in0=buf[:, sl],
                                                     scalar1=wsb[:, 7 + mi:8 + mi],
                                                     scalar2=None, op0=ALU.is_lt),
                                     "wq_" + mat)
                        ctx.wait(v, "dve", t2)
                        q = v.tensor_sub(wqbf[gidx % 2][:, sl], accA[:, 0:min(TOK, w)],
                                         accB[:, 0:min(TOK, w)])
                        tq = ctx.inc(q, "wq_" + mat)
                    wq_reqs.append((mat, si, tq))
                    gidx += 1

        @blk.gpsimd
        def _(g):
            for gi, (mat, si, tq) in enumerate(wq_reqs):
                ctx.wait(g, "gps", tq)
                ctx.inc(g.dma_start(out=wq_in_d[mat][128 * si:128 * (si + 1), :],
                                    in_=wqbf[gi % 2][:, 0:wwid[mat]]), f"dqs{gi % 2}", 16)
            ctx.wv(g, "gps", "dqs0", ctx.n("dqs0"))
            ctx.wv(g, "gps", "dqs1", ctx.n("dqs1"))
            for mat in ("g", "u", "d"):
                cc = g.collective_compute(
                    "AllGather", ALU.bypass, replica_groups=[list(range(NCORE))],
                    ins=[wq_in_d[mat].rearrange("a b -> (a b)")[:]],
                    outs=[wq_out_d[mat].rearrange("r a b -> (r a b)")[:]])
                ctx.inc(cc, "ag_" + mat)
        tok_ag = {m: ctx.tok("ag_" + m) for m in ("g", "u", "d")}

        ctx.barrier()
        wscope.close()

        def ffn_pass(pid):
            stop = FFN_STOP
            fsc = Scope()
            qscr["accC"] = fsc.sbuf(f"accC_{pid}", [128, TOK], F32)
            qscr["accD"] = fsc.sbuf(f"accD_{pid}", [128, TOK], F32)
            qscr["accI"] = fsc.sbuf(f"accI_{pid}", [128, TOK], I32)
            qscr["bcC"] = fsc.sbuf(f"bcC_{pid}", [128, TOK], F32)
            accC, accD, accI, bcC = (qscr["accC"], qscr["accD"],
                                     qscr["accI"], qscr["bcC"])
            wslab = [fsc.sbuf(f"wslab{i}_{pid}", [128, EC * JG], BF16) for i in range(4)]
            hgrp = fsc.sbuf(f"hgrp_{pid}", [128, TOK], F32)
            sg_t = fsc.sbuf(f"sg_t_{pid}", [128, TT], F32)
            m_run = fsc.sbuf(f"m_run_{pid}", [128, TOK], F32)
            m_run2 = fsc.sbuf(f"m_run2_{pid}", [128, TOK], F32)
            hrd = [fsc.sbuf(f"hrd{i}_{pid}", [128, TT], F32) for i in range(2)]
            dwslab = [fsc.sbuf(f"dwslab{i}_{pid}", [128, 1024], BF16) for i in range(4)]
            q2sl = [fsc.sbuf(f"q2sl{i}_{pid}", [128, TT], BF16) for i in range(4)]

            tok_xq = emit_ln(x, xdq, True,
                             [ctx.tok("d_x"), ctx.tok("dep"), ctx.tok("o_ep")])
            ctx.barrier()
            if stop == "xq":
                fsc.close()
                return

            wgu_g = {"g": wq_out_d["g"], "u": wq_out_d["u"]}
            wsl = {"g": (wslab[0], wslab[1]), "u": (wslab[2], wslab[3])}
            b_dgu = {m + s: ctx.n(f"dgu{m}{s}") for m in ("g", "u") for s in ("0", "1")}
            b_gu = ctx.n("gu_g")
            b_guu = ctx.n("gu_u")
            b_sil = ctx.n("sil")
            b_h = ctx.n("h_ok")
            b_hw = ctx.n("h_w")
            b_hsp = [ctx.n("d_hsp0"), ctx.n("d_hsp1")]

            @blk.sync
            def _(sp):
                for mat in ("g", "u"):
                    ctx.wait(sp, "sp", tok_ag[mat])
                for jg in range(NJG):
                    r, pp = jg // 8, jg % 8
                    for mat in ("g", "u"):
                        if jg >= 2:
                            ctx.wv(sp, "sp", "gu_u", b_guu + 2 * (jg - 1))
                        src = wgu_g[mat][r][:, pp * JG:(pp + 1) * JG]
                        ctx.inc(sp.dma_start(
                            out=wsl[mat][jg % 2][:, :].rearrange(
                                "p (ic j) -> p ic j", j=JG),
                            in_=src.rearrange("(ic p) j -> p ic j", p=128)),
                            f"dgu{mat}{jg % 2}", 16)

            @blk.tensor
            def _(t):
                ctx.wait(t, "pe", tok_xq)
                for jg in range(NJG):
                    for tt2 in range(2):
                        unit = jg * 2 + tt2
                        bg = ps[(unit % 2) * 2]
                        bu = ps[(unit % 2) * 2 + 1]
                        ctx.wv(t, "pe", f"dgug{jg % 2}",
                               b_dgu["g" + str(jg % 2)] + 16 * (jg // 2 + 1))
                        ctx.wv(t, "pe", f"dguu{jg % 2}",
                               b_dgu["u" + str(jg % 2)] + 16 * (jg // 2 + 1))
                        if unit >= 2:
                            t.wait_ge(ctx.sem("h_ok"), b_h + 2 + 2 * (unit - 1))
                        for ic in range(EC):
                            mm = t.matmul(bg[:, 0:TT],
                                          wsl["g"][jg % 2][:, ic * JG:(ic + 1) * JG],
                                          xdq[:, ic * TOK + tt2 * TT:ic * TOK + tt2 * TT + TT],
                                          start=(ic == 0), stop=(ic == EC - 1))
                        ctx.inc(mm, "gu_g")
                        for ic in range(EC):
                            mm = t.matmul(bu[:, 0:TT],
                                          wsl["u"][jg % 2][:, ic * JG:(ic + 1) * JG],
                                          xdq[:, ic * TOK + tt2 * TT:ic * TOK + tt2 * TT + TT],
                                          start=(ic == 0), stop=(ic == EC - 1))
                        ctx.inc(mm, "gu_u")

            @blk.scalar
            def _(s):
                for jg in range(NJG):
                    for tt2 in range(2):
                        unit = jg * 2 + tt2
                        bg = ps[(unit % 2) * 2]
                        ctx.wv(s, "act", "gu_g", b_gu + unit + 1)
                        if unit >= 1:
                            s.wait_ge(ctx.sem("h_w"), b_hw + unit)
                        a = s.activation(out=sg_t[:, 0:TT], in_=bg[:, 0:TT],
                                         func=AF.Silu, scale=wsb[:, 0:1])
                        ctx.inc(a, "sil")

            @blk.vector
            def _(v):
                ctx.inc(v.memset(m_run[:, :], 0.0), "h_ok")
                ctx.inc(v.memset(m_run2[:, :], 0.0), "h_ok")
                b_h2 = b_h + 1
                for jg in range(NJG):
                    for tt2 in range(2):
                        unit = jg * 2 + tt2
                        bu = ps[(unit % 2) * 2 + 1]
                        ctx.wv(v, "dve", "sil", b_sil + unit + 1)
                        ctx.wv(v, "dve", "gu_u", b_guu + unit + 1)
                        if jg >= 1:
                            ctx.wv(v, "dve", f"d_hsp{tt2}", b_hsp[tt2] + 16 * jg)
                        hsl = hgrp[:, tt2 * TT:(tt2 + 1) * TT]
                        hh_ = v.scalar_tensor_tensor(out=hsl, in0=bu[:, 0:TT],
                                                     scalar=wsb[:, 1:2], in1=sg_t[:, 0:TT],
                                                     op0=ALU.mult, op1=ALU.mult)
                        th = ctx.inc(hh_, "h_w")
                        ctx.wait(v, "dve", th)
                        ctx.wait(v, "dve", ctx.tok("h_ok"))
                        mr = v.tensor_max(m_run[:, tt2 * TT:(tt2 + 1) * TT],
                                          m_run[:, tt2 * TT:(tt2 + 1) * TT], hsl)
                        ctx.inc(mr, "h_ok")
                        mr2 = v.tensor_tensor(m_run2[:, tt2 * TT:(tt2 + 1) * TT],
                                              m_run2[:, tt2 * TT:(tt2 + 1) * TT], hsl,
                                              op=ALU.min)
                        ctx.inc(mr2, "h_ok")

            @blk.gpsimd
            def _(g):
                for jg in range(NJG):
                    for tt2 in range(2):
                        unit = jg * 2 + tt2
                        ctx.wv(g, "gps", "h_ok", b_h + 2 + 2 * (unit + 1))
                        ctx.inc(g.dma_start(out=h_dram[tt2, jg],
                                            in_=hgrp[:, tt2 * TT:(tt2 + 1) * TT]),
                                f"d_hsp{tt2}", 16)
            tok_mrun = ctx.tok("h_ok")
            tok_hsp = [ctx.tok("d_hsp0"), ctx.tok("d_hsp1")]
            ctx.barrier()
            if stop == "gu":
                fsc.close()
                return

            @blk.vector
            def _(v):
                t1 = ctx.inc(v.tensor_scalar(out=m_run2[:, :], in0=m_run2[:, :],
                                             scalar1=-1.0, scalar2=None,
                                             op0=ALU.mult), "h_ok")
                ctx.wait(v, "dve", t1)
                ctx.inc(v.tensor_max(m_run[:, :], m_run[:, :], m_run2[:, :]), "h_ok")
            tok_mcmb = ctx.tok("h_ok")
            ctx.barrier()

            b_htr = ctx.n("htr")
            b_hred = ctx.n("hred")
            b_htr2 = ctx.n("htr2")
            b_hrow = ctx.n("hrow")

            @blk.tensor
            def _(t):
                ctx.wait(t, "pe", tok_mcmb)
                for tb in range(8):
                    if tb >= 2:
                        t.wait_ge(ctx.sem("hred"), b_hred + tb - 1)
                    mm = t.transpose(ps[4 + tb % 2][:, 0:128],
                                     m_run[:, tb * 128:(tb + 1) * 128], identf[:, :])
                    ctx.inc(mm, "htr")

            @blk.vector
            def _(v):
                for tb in range(8):
                    ctx.wv(v, "dve", "htr", b_htr + tb + 1)
                    ctx.inc(v.reduce_max(colA[:, tb:tb + 1], ps[4 + tb % 2][:, 0:128],
                                         axis=AXX), "hred")
            tok_hred = ctx.tok("hred")

            @blk.tensor
            def _(t):
                ctx.wait(t, "pe", tok_hred)
                for tb in range(8):
                    off = (tb % 4) * 128
                    if tb == 4:
                        t.wait_ge(ctx.sem("hrow"), b_hrow + 1)
                    ctx.inc(t.transpose(ps[6][0:1, off:off + 128],
                                        colA[:, tb:tb + 1], identf[:, :]), "htr2")

            @blk.vector
            def _(v):
                for grp in range(2):
                    ctx.wv(v, "dve", "htr2", b_htr2 + 4 * (grp + 1))
                    ctx.inc(v.tensor_copy(rowC[0:1, grp * 512:(grp + 1) * 512],
                                          ps[6][0:1, 0:512]), "hrow")
                ctx.wait(v, "dve", ctx.tok("hrow"))
                ctx.inc(v.tensor_scalar(out=rowC[0:1, :], in0=rowC[0:1, :],
                                        scalar1=INV_127, scalar2=Q_EPS,
                                        op0=ALU.mult, op1=ALU.add), "hrow")
                ctx.wait(v, "dve", ctx.tok("hrow"))
                ctx.inc(v.reciprocal(rowD[0:1, :], rowC[0:1, :]), "hrow")
            tok_h2 = emit_bcast([(rowC, bcA), (rowD, bcB)], ctx.tok("hrow"))
            if stop == "hscale":
                ctx.barrier()
                fsc.close()
                return

            b_hrd = [ctx.n("d_hrd0"), ctx.n("d_hrd1")]
            b_q2 = ctx.n("q2")
            b_q2s = [ctx.n("d_q2s0"), ctx.n("d_q2s1")]

            @blk.sync
            def _(sp):
                for tk in tok_hsp:
                    ctx.wait(sp, "sp", tk)
                for k in range(2 * JC):
                    tt2, jc = k // JC, k % JC
                    if k >= 2:
                        ctx.wv(sp, "sp", "q2", b_q2 + 6 * (k - 1))
                    ctx.inc(sp.dma_start(out=hrd[k % 2][:, :], in_=h_dram[tt2, jc]),
                            f"d_hrd{k % 2}", 16)

            @blk.vector
            def _(v):
                ctx.wait(v, "dve", tok_h2)
                for k in range(2 * JC):
                    tt2 = k // JC
                    bA = bcA[:, tt2 * TT:(tt2 + 1) * TT]
                    bB = bcB[:, tt2 * TT:(tt2 + 1) * TT]
                    buf = hrd[k % 2]
                    ctx.wv(v, "dve", f"d_hrd{k % 2}", b_hrd[k % 2] + 16 * (k // 2 + 1))
                    if k >= 2:
                        ctx.wv(v, "dve", f"d_q2s{k % 2}", b_q2s[k % 2] + 16 * (k // 2))
                    t1 = ctx.inc(v.tensor_mul(accA[:, 0:TT], buf[:, :], bB), "q2")
                    ctx.wait(v, "dve", t1)
                    t2 = ctx.inc(v.tensor_scalar(out=accI[:, 0:TT], in0=accA[:, 0:TT],
                                                 scalar1=1024.0, scalar2=None,
                                                 op0=ALU.add), "q2")
                    ctx.wait(v, "dve", t2)
                    t3 = ctx.inc(v.scalar_tensor_tensor(out=accB[:, 0:TT], in0=accI[:, 0:TT],
                                                        scalar=-1.0, in1=accA[:, 0:TT],
                                                        op0=ALU.mult, op1=ALU.add), "q2")
                    ctx.wait(v, "dve", t3)
                    t4 = ctx.inc(v.tensor_scalar(out=accB[:, 0:TT], in0=accB[:, 0:TT],
                                                 scalar1=-1023.5, scalar2=None,
                                                 op0=ALU.is_ge), "q2")
                    ctx.wait(v, "dve", t4)
                    t5 = ctx.inc(v.scalar_tensor_tensor(out=accC[:, 0:TT], in0=accI[:, 0:TT],
                                                        scalar=-1024.0, in1=accB[:, 0:TT],
                                                        op0=ALU.add, op1=ALU.add), "q2")
                    ctx.wait(v, "dve", t5)
                    ctx.inc(v.tensor_mul(wqbf[k % 2][:, 0:TT], accC[:, 0:TT], bA), "q2")

            @blk.gpsimd
            def _(g):
                for k in range(2 * JC):
                    tt2, jc = k // JC, k % JC
                    ctx.wv(g, "gps", "q2", b_q2 + 6 * (k + 1))
                    ctx.inc(g.dma_start(out=q2_dram[tt2, jc], in_=wqbf[k % 2][:, 0:TT]),
                            f"d_q2s{k % 2}", 16)
            tok_q2s = [ctx.tok("d_q2s0"), ctx.tok("d_q2s1")]
            ctx.barrier()
            if stop == "q2":
                fsc.close()
                return

            b_ddw = [ctx.n(f"d_dw{i}") for i in range(4)]
            b_dq2 = [ctx.n(f"d_q2r{i}") for i in range(4)]
            b_dpe = ctx.n("dpe")
            b_dep = ctx.n("dep")

            @blk.sync
            def _(sp):
                ctx.wait(sp, "sp", tok_ag["d"])
                for sw in range(4):
                    tt2, hh = sw // 2, sw % 2
                    for jc in range(JC):
                        k = sw * JC + jc
                        r, pp = jc // 8, jc % 8
                        if k >= 4:
                            ctx.wv(sp, "sp", "dpe", b_dpe + (k - 3))
                        src = wq_out_d["d"][r][pp * 128:(pp + 1) * 128,
                                               hh * 1024:(hh + 1) * 1024]
                        ctx.inc(sp.dma_start(out=dwslab[k % 4][:, :], in_=src),
                                f"d_dw{k % 4}", 16)

            @blk.gpsimd
            def _(g):
                for tk in tok_q2s:
                    ctx.wait(g, "gps", tk)
                for sw in range(4):
                    tt2 = sw // 2
                    for jc in range(JC):
                        k = sw * JC + jc
                        if k >= 4:
                            ctx.wv(g, "gps", "dpe", b_dpe + (k - 3))
                        ctx.inc(g.dma_start(out=q2sl[k % 4][:, :], in_=q2_dram[tt2, jc]),
                                f"d_q2r{k % 4}", 16)

            @blk.tensor
            def _(t):
                for sw in range(4):
                    tt2, hh = sw // 2, sw % 2
                    for jc in range(JC):
                        k = sw * JC + jc
                        ctx.wv(t, "pe", f"d_q2r{k % 4}", b_dq2[k % 4] + 16 * (k // 4 + 1))
                        ctx.wv(t, "pe", f"d_dw{k % 4}", b_ddw[k % 4] + 16 * (k // 4 + 1))
                        if jc == 0 and sw >= 1:
                            t.wait_ge(ctx.sem("dep"), b_dep + 8 * sw)
                        mm = None
                        for ho in range(8):
                            mm = t.matmul(ps[ho][:, 0:TT],
                                          dwslab[k % 4][:, ho * 128:(ho + 1) * 128],
                                          q2sl[k % 4][:, :],
                                          start=(jc == 0), stop=(jc == JC - 1))
                        ctx.inc(mm, "dpe")

            @blk.vector
            def _(v):
                for sw in range(4):
                    tt2, hh = sw // 2, sw % 2
                    ctx.wv(v, "dve", "dpe", b_dpe + JC * (sw + 1))
                    for ho in range(8):
                        ch = hh * 8 + ho
                        e = v.scalar_tensor_tensor(
                            out=x[ch][:, tt2 * TT:(tt2 + 1) * TT],
                            in0=ps[ho][:, 0:TT], scalar=wsb[:, 3:4],
                            in1=x[ch][:, tt2 * TT:(tt2 + 1) * TT],
                            op0=ALU.mult, op1=ALU.add)
                        ctx.inc(e, "dep")
            ctx.barrier()
            fsc.close()

        def attn_phase():
            qsc = Scope()
            cos_sb = qsc.sbuf("cos_sb", [128, TOK], F32)
            sin_sb = qsc.sbuf("sin_sb", [128, TOK], F32)
            qkslab = [qsc.sbuf(f"qkslab{i}", [128, EC * 128], BF16) for i in range(2)]
            vslab = qsc.sbuf("vslab", [128, EC * 512], BF16)
            att_st = qsc.sbuf("att_st", [128, TOK], F32)
            att_sw = qsc.sbuf("att_sw", [128, TOK], F32)
            att_bf = qsc.sbuf("att_bf", [128, TOK], BF16)
            vevac = qsc.sbuf("vevac", [128, TT], BF16)
            bvb = qsc.sbuf("bvb", [128, E], BF16)

            @blk.sync
            def _(sp):
                ctx.inc(sp.dma_start(out=cos_sb[:], in_=cosr[:, :]), "d_c", 16)
                ctx.inc(sp.dma_start(out=sin_sb[:], in_=sinr[:, :]), "d_c", 16)
            tok_cs = ctx.tok("d_c")

            tok_ln2 = emit_ln(x, xdq, False, [ctx.tok("d_x"), ctx.tok("dep")])
            ctx.barrier()

            @blk.sync
            def _(sp):
                ctx.inc(sp.dma_start(out=rowA[0:1, :],
                                     in_=b_attn["wv"][0:TOK].rearrange("a one -> one a")),
                        "d_bv", 16)
                ctx.inc(sp.dma_start(out=rowB[0:1, :],
                                     in_=b_attn["wv"][TOK:E].rearrange("a one -> one a")),
                        "d_bv", 16)
            tok_bvr = ctx.tok("d_bv")
            b_bv = ctx.n("bc_dv")

            @blk.tensor
            def _(t):
                ctx.wait(t, "pe", tok_bvr)
                for q in range(4):
                    if q >= 2:
                        t.wait_ge(ctx.sem("bc_dv"), b_bv + q - 1)
                    row = rowA if q < 2 else rowB
                    mm = t.matmul(ps[6 + (q % 2)][:, 0:512], ones1f[0:1, 0:128],
                                  row[0:1, (q % 2) * 512:(q % 2 + 1) * 512],
                                  start=True, stop=True)
                    ctx.inc(mm, "bc_pe")
            b_bvp = ctx.n("bc_pe")

            @blk.vector
            def _(v):
                for q in range(4):
                    ctx.wv(v, "dve", "bc_pe", b_bvp - 4 + q + 1)
                    ctx.inc(v.tensor_copy(bvb[:, q * 512:(q + 1) * 512],
                                          ps[6 + (q % 2)][:, 0:512]), "bc_dv")
            tok_bv = ctx.tok("bc_dv")

            for mat_i, (mat, wdr, bcol, stage_d) in enumerate(
                    (("q", w_attn["wq"], 0, qag_in), ("k", w_attn["wk"], 1, kag_in))):
                b_wl = [ctx.n("d_qk0"), ctx.n("d_qk1")]
                b_pe = ctx.n("qk_pe")
                b_ev = ctx.n("qk_ev")
                b_sw = [ctx.n("d_swap0"), ctx.n("d_swap1")]
                b_ro = ctx.n("rope")
                b_st = [ctx.n("d_stg0"), ctx.n("d_stg1")]

                @blk.gpsimd
                def _(g):
                    for h in range(H):
                        hh = mat_i * H + h
                        if h >= 2:
                            ctx.wv(g, "gps", "qk_pe", b_pe + 2 * (h - 1))
                        elif mat_i == 1:
                            ctx.wv(g, "gps", "qk_pe", b_pe)
                        src = wdr[:, h * 128:(h + 1) * 128]
                        ctx.inc(g.dma_start(
                            out=qkslab[hh % 2][:, :].rearrange("p (ic e) -> p ic e", e=128),
                            in_=src.rearrange("(ic p) e -> p ic e", p=128)),
                            f"d_qk{hh % 2}", 16)

                @blk.tensor
                def _(t):
                    ctx.wait(t, "pe", ctx.tok("ln_app"))
                    for h in range(H):
                        hh = mat_i * H + h
                        ctx.wv(t, "pe", f"d_qk{h % 2}",
                               b_wl[h % 2] + 16 * (h // 2 + 1))
                        if h >= 1:
                            t.wait_ge(ctx.sem("qk_ev"), b_ev + 2 * h)
                        elif mat_i == 1:
                            ctx.wv(t, "pe", "qk_ev", b_ev)
                        for tt2 in range(2):
                            for ic in range(EC):
                                mm = t.matmul(
                                    ps[tt2][:, 0:TT],
                                    qkslab[hh % 2][:, ic * 128:(ic + 1) * 128],
                                    xdq[:, ic * TOK + tt2 * TT:ic * TOK + tt2 * TT + TT],
                                    start=(ic == 0), stop=(ic == EC - 1))
                            ctx.inc(mm, "qk_pe")

                @blk.scalar
                def _(s):
                    for h in range(H):
                        hh = mat_i * H + h
                        if h >= 1:
                            s.wait_ge(ctx.sem("rope"), b_ro + 3 * (h - 1) + 2)
                            ctx.wv(s, "act", "rope", b_ro + 3 * h)
                            ctx.inc(s.dma_start(out=stage_d[h - 1], in_=att_bf[:, :]),
                                    f"d_stg{(h - 1) % 2}", 16)
                        for tt2 in range(2):
                            ctx.wv(s, "act", "qk_pe", b_pe + 2 * h + tt2 + 1)
                            a = s.activation(out=att_st[:, tt2 * TT:(tt2 + 1) * TT],
                                             in_=ps[tt2][:, 0:TT], func=AF.Identity,
                                             bias=bias_sb[:, bcol * EC + h:bcol * EC + h + 1],
                                             scale=1.0)
                            ctx.inc(a, "qk_ev")
                    ctx.wv(s, "act", "rope", b_ro + 3 * H)
                    ctx.inc(s.dma_start(out=stage_d[H - 1], in_=att_bf[:, :]),
                            f"d_stg{(H - 1) % 2}", 16)

                @blk.sync
                def _(sp):
                    for h in range(H):
                        ctx.wv(sp, "sp", "qk_ev", b_ev + 2 * (h + 1))
                        if h >= 1:
                            ctx.wv(sp, "sp", "rope", b_ro + 3 * h)
                        ctx.inc(sp.dma_start(out=att_sw[0:64, :], in_=att_st[64:128, :]),
                                f"d_swap{h % 2}", 16)
                        ctx.inc(sp.dma_start(out=att_sw[64:128, :], in_=att_st[0:64, :]),
                                f"d_swap{h % 2}", 16)

                @blk.vector
                def _(v):
                    ctx.wait(v, "dve", tok_cs)
                    for h in range(H):
                        ctx.wv(v, "dve", f"d_swap{h % 2}", b_sw[h % 2] + 32 * (h // 2 + 1))
                        if h >= 1:
                            ctx.wv(v, "dve", f"d_stg{(h - 1) % 2}",
                                   b_st[(h - 1) % 2] + 16 * ((h - 1) // 2 + 1))
                        elif mat_i == 1:
                            ctx.wv(v, "dve", "d_stg0", b_st[0])
                            ctx.wv(v, "dve", "d_stg1", b_st[1])
                        t1 = ctx.inc(v.tensor_mul(att_sw[:, :], att_sw[:, :], sin_sb[:, :]),
                                     "rope")
                        ctx.wait(v, "dve", t1)
                        t2 = ctx.inc(v.tensor_mul(att_st[:, :], att_st[:, :],
                                                  cos_sb[:, :]), "rope")
                        ctx.wait(v, "dve", t2)
                        t3 = ctx.inc(v.tensor_add(att_bf[:, :], att_st[:, :], att_sw[:, :]),
                                     "rope")


            ctx.barrier()

            kq_tail = [ctx.tok(n) for n in ("d_stg0", "d_stg1", "d_qk0", "d_qk1")]

            @blk.gpsimd
            def _(g):
                for tk in kq_tail:
                    ctx.wait(g, "gps", tk)
                for p in range(2):
                    for (src, dst) in ((kag_in, kag_out), (qag_in, qag_out)):
                        cc = g.collective_compute(
                            "AllGather", ALU.bypass, replica_groups=[list(range(NCORE))],
                            ins=[src[8 * p:8 * (p + 1)].rearrange("h a b -> (h a b)")[:]],
                            outs=[dst[p].rearrange("r h a b -> (r h a b)")[:]])
                        ctx.inc(cc, f"ag_kqv{p}")

            b_vl = ctx.n("d_vl")
            b_vpe = ctx.n("v_pe")
            b_vev = ctx.n("v_ev")
            b_vst = [ctx.n("d_vst0"), ctx.n("d_vst1")]

            @blk.gpsimd
            def _(g):
                for et in range(4):
                    if et >= 1:
                        ctx.wv(g, "gps", "v_pe", b_vpe + 8 * et)
                    src = w_attn["wv"][:, et * 512:(et + 1) * 512]
                    ctx.inc(g.dma_start(
                        out=vslab[:, :].rearrange("p (ic e) -> p ic e", e=512),
                        in_=src.rearrange("(ic p) e -> p ic e", p=128)), "d_vl", 16)

            @blk.tensor
            def _(t):
                ctx.wait(t, "pe", ctx.tok("qk_ev"))
                for et in range(4):
                    ctx.wv(t, "pe", "d_vl", b_vl + 16 * (et + 1))
                    for tc in range(8):
                        u = et * 8 + tc
                        if u >= 2:
                            t.wait_ge(ctx.sem("v_ev"), b_vev + u - 1)
                        for ic in range(EC):
                            mm = t.matmul(ps[u % 2][:, 0:TT],
                                          xdq[:, ic * TOK + tc * 128:ic * TOK + (tc + 1) * 128],
                                          vslab[:, ic * 512:(ic + 1) * 512],
                                          start=(ic == 0), stop=(ic == EC - 1))
                        ctx.inc(mm, "v_pe")

            @blk.vector
            def _(v):
                ctx.wait(v, "dve", tok_bv)
                for et in range(4):
                    for tc in range(8):
                        u = et * 8 + tc
                        ctx.wv(v, "dve", "v_pe", b_vpe + u + 1)
                        if u >= 1:
                            ctx.wv(v, "dve", f"d_vst{(u - 1) % 2}",
                                   b_vst[(u - 1) % 2] + 64 * ((u - 1) // 2 + 1))
                        e = v.tensor_add(vevac[:, :], ps[u % 2][:, 0:TT],
                                         bvb[:, et * 512:(et + 1) * 512])
                        ctx.inc(e, "v_ev")

            @blk.sync
            def _(sp):
                for et in range(4):
                    for tc in range(8):
                        u = et * 8 + tc
                        ctx.wv(sp, "sp", "v_ev", b_vev + u + 1)
                        for hq in range(4):
                            h = et * 4 + hq
                            ctx.inc(sp.dma_start(
                                out=vag_in[h][tc * 128:(tc + 1) * 128, :],
                                in_=vevac[:, hq * 128:(hq + 1) * 128]), f"d_vst{u % 2}", 16)
            ctx.barrier()

            qsc.close()
            flsc = Scope()
            kh = flsc.sbuf("kh", [128, S], BF16)
            qh = flsc.sbuf("qh", [128, S], BF16)
            vexth = flsc.sbuf("vexth", [128, 64 * 129], BF16)
            attn_sb = [flsc.sbuf(f"attn_sb{i}", [128, TT], BF16) for i in range(2)]
            otok = flsc.sbuf("otok", [128, 128], F32)
            obuf = flsc.sbuf("obuf", [128, TT], BF16)

            qkv_tail = [ctx.tok(n) for n in
                        ("d_stg0", "d_stg1", "d_vst0", "d_vst1", "d_swap0",
                         "d_swap1", "d_qk0", "d_qk1", "d_vl", "d_bv",
                         "v_ev", "rope", "bc_dv")]

            @blk.vector
            def _(v):
                for tk in qkv_tail:
                    ctx.wait(v, "dve", tk)
                ctx.inc(v.memset(vexth[:], 1.0), "s_init")
            tok_vx = ctx.tok("s_init")

            @blk.gpsimd
            def _(g):
                for tk in qkv_tail:
                    ctx.wait(g, "gps", tk)
                for p in range(2):
                    cc = g.collective_compute(
                        "AllGather", ALU.bypass, replica_groups=[list(range(NCORE))],
                        ins=[vag_in[8 * p:8 * (p + 1)].rearrange("h a b -> (h a b)")[:]],
                        outs=[vag_out[p].rearrange("r h a b -> (r h a b)")[:]])
                    ctx.inc(cc, f"ag_kqv{p}")
            tok_agkqv = [ctx.tok("ag_kqv0"), ctx.tok("ag_kqv1")]

            for hi in range(HPC):
                b_kl = ctx.n("d_kh")
                b_sc = ctx.n("fl_sc")
                b_ex = ctx.n("fl_ex")
                b_mk = ctx.n("fl_mk")
                b_av = ctx.n("fl_av")
                b_fin = ctx.n("fl_fin")
                b_tr = ctx.n("fl_tr")
                b_oe = ctx.n("fl_oe")
                b_ost = [ctx.n("d_ost0"), ctx.n("d_ost1")]
                b_msq = ctx.n("m_sq")
                b_mmm = ctx.n("m_mm")
                b_mrd = ctx.n("m_rd")

                @blk.gpsimd
                def _(g):
                    for nm, val in tok_agkqv:
                        ctx.wv(g, "gps", nm, val - 1)
                    for arm in g.switch_core_id(128):
                        my = arm.logical % 8
                        h = my * HPC + hi
                        p, hs = h // 8, h % 8
                        for r in range(NCORE):
                            g.dma_start(out=kh[:, r * TOK:(r + 1) * TOK],
                                        in_=kag_out[p, r, hs]).then_inc(ctx.sem("d_kh"), 16)
                            g.dma_start(out=qh[:, r * TOK:(r + 1) * TOK],
                                        in_=qag_out[p, r, hs]).then_inc(ctx.sem("d_kh"), 16)
                    ctx.counts[ctx._r("d_kh")] += 16 * 16
                    ctx.wait(g, "gps", tok_vx)
                    for tk in tok_agkqv:
                        ctx.wait(g, "gps", tk)
                    for arm in g.switch_core_id(128):
                        my = arm.logical % 8
                        h = my * HPC + hi
                        p, hs = h // 8, h % 8
                        for r in range(NCORE):
                            g.dma_start(
                                out=vexth[:, :].rearrange("p (c n) -> p c n", n=129)[
                                    :, r * 8:(r + 1) * 8, 0:128],
                                in_=vag_out[p, r, hs].rearrange("(c p) d -> p c d", p=128)
                            ).then_inc(ctx.sem("d_kh"), 16)
                    ctx.counts[ctx._r("d_kh")] += 16 * 8
                tok_kh = ctx.tok("d_kh")

                @blk.scalar
                def _(s):
                    ctx.wv(s, "act", "d_kh", b_kl + 16 * 16)
                    a = s.activation(out=xdq[:, 0:S], in_=qh[:, :], func=AF.Square)
                    ctx.inc(a, "m_sq")
                    a = s.activation(out=xdq[:, S:2 * S], in_=kh[:, :], func=AF.Square)
                    ctx.inc(a, "m_sq")
                tok_msq = ctx.tok("m_sq")

                @blk.tensor
                def _(t):
                    ctx.wait(t, "pe", tok_msq)
                    for z in range(32):
                        if z >= 2:
                            t.wait_ge(ctx.sem("m_rd"), b_mrd + z - 1)
                        mm = t.matmul(ps[6 + (z % 2)][0:1, 0:512], ones128b[:, 0:1],
                                      xdq[:, z * 512:(z + 1) * 512], start=True, stop=True)
                        ctx.inc(mm, "m_mm")

                @blk.vector
                def _(v):
                    for z in range(32):
                        ctx.wv(v, "dve", "m_mm", b_mmm + z + 1)
                        r = v.reduce_max(rowE[0:1, 32 + z:33 + z],
                                         ps[6 + (z % 2)][0:1, 0:512], axis=AXX)
                        ctx.inc(r, "m_rd")
                    ctx.wait(v, "dve", ctx.tok("m_rd"))
                    r = v.reduce_max(rowE[0:1, 20:21], rowE[0:1, 32:48], axis=AXX)
                    t1 = ctx.inc(r, "m_rd")
                    r = v.reduce_max(rowE[0:1, 21:22], rowE[0:1, 48:64], axis=AXX)
                    t2 = ctx.inc(r, "m_rd")
                    ctx.wait(v, "dve", t2)
                    t3 = ctx.inc(v.tensor_mul(rowE[0:1, 20:21], rowE[0:1, 20:21],
                                              rowE[0:1, 21:22]), "m_rd")
                tok_mrd = ctx.tok("m_rd")

                @blk.scalar
                def _(s):
                    ctx.wait(s, "act", tok_mrd)
                    a = s.activation(out=rowE[0:1, 20:21], in_=rowE[0:1, 20:21],
                                     func=AF.Sqrt, scale=1.0 / 128.0)
                    ctx.inc(a, "m_rd2")
                tok_mrd2 = ctx.tok("m_rd2")

                @blk.vector
                def _(v):
                    ctx.wait(v, "dve", tok_mrd2)
                    ctx.inc(v.tensor_scalar(out=rowE[0:1, 20:21], in0=rowE[0:1, 20:21],
                                            scalar1=1.0, scalar2=-1.0,
                                            op0=ALU.add, op1=ALU.mult), "m_rd3")
                tok_mrd3 = ctx.tok("m_rd3")

                @blk.tensor
                def _(t):
                    ctx.wait(t, "pe", tok_mrd3)
                    mm = t.matmul(ps[6][:, 0:1], ones1f[0:1, 0:128], rowE[0:1, 20:21],
                                  start=True, stop=True)
                    ctx.inc(mm, "m_bc")
                tok_mbc = ctx.tok("m_bc")

                @blk.vector
                def _(v):
                    ctx.wait(v, "dve", tok_mbc)
                    ctx.inc(v.tensor_copy(mneg[:, hi:hi + 1], ps[6][:, 0:1]), "m_ok")
                tok_mok = ctx.tok("m_ok")

                units = []
                for u in range(16):
                    for kb in range(u + 1):
                        for kkc in range(4):
                            units.append((u, kb, kkc))
                n_units = len(units)

                @blk.tensor
                def _(t):
                    ctx.wait(t, "pe", tok_mok)
                    ctx.wait(t, "pe", tok_kh)
                    ui = 0
                    for u in range(16):
                        if u >= 1:
                            t.wait_ge(ctx.sem("fl_fin"), b_fin + 8 * u)
                        for kb in range(u + 1):
                            for kkc in range(4):
                                ui += 1
                                bsc = ps[ui % 2]
                                if ui > 2:
                                    t.wait_ge(ctx.sem("fl_ex"), b_ex + ui - 2)
                                kcol = (kb * 4 + kkc) * 128
                                mm = t.matmul(bsc[:, 0:512], kh[:, kcol:kcol + 128],
                                              qh[:, u * 512:(u + 1) * 512],
                                              start=True, stop=True)
                                ctx.inc(mm, "fl_sc")
                                diag = (kb == u)
                                need = ("fl_mk", b_mk + (u * 4 + kkc + 1)) if diag else \
                                       ("fl_ex", b_ex + ui)
                                t.wait_ge(ctx.sem(need[0]), need[1])
                                q4s = range(kkc, 4) if diag else range(4)
                                mm2 = None
                                for q4 in q4s:
                                    st = (kb == 0 and kkc == 0)
                                    sp_ = diag and (kkc == q4)
                                    mm2 = t.matmul(ps[2 + q4][:, 0:129],
                                                   attn_sb[ui % 2][:, q4 * 128:(q4 + 1) * 128],
                                                   vexth[:, (kb * 4 + kkc) * 129:
                                                         (kb * 4 + kkc + 1) * 129],
                                                   start=st, stop=sp_)
                                ctx.inc(mm2, "fl_av")
                        for q4 in range(4):
                            t.wait_ge(ctx.sem("fl_fin"), b_fin + 8 * u + 2 * (q4 + 1))
                            if u * 4 + q4 >= 1:
                                t.wait_ge(ctx.sem("fl_oe"), b_oe + u * 4 + q4)
                            mm = t.transpose(ps[6][:, 0:128], otok[:, :], identf[:, :])
                            ctx.inc(mm, "fl_tr")

                @blk.scalar
                def _(s):
                    ui = 0
                    for u in range(16):
                        for kb in range(u + 1):
                            for kkc in range(4):
                                ui += 1
                                bsc = ps[ui % 2]
                                ctx.wv(s, "act", "fl_sc", b_sc + ui)
                                if ui > 2:
                                    s.wait_ge(ctx.sem("fl_av"), b_av + ui - 2)
                                a = s.activation(out=attn_sb[ui % 2][:, :], in_=bsc[:, 0:512],
                                                 func=AF.Exp, scale=INV_SQRT_D,
                                                 bias=mneg[:, hi:hi + 1])
                                ctx.inc(a, "fl_ex")

                @blk.vector
                def _(v):
                    ui = 0
                    for u in range(16):
                        for kb in range(u + 1):
                            for kkc in range(4):
                                ui += 1
                                if kb == u:
                                    ctx.wv(v, "dve", "fl_ex", b_ex + ui)
                                    mk = v.tensor_mul(
                                        attn_sb[ui % 2][:, kkc * 128:(kkc + 1) * 128],
                                        attn_sb[ui % 2][:, kkc * 128:(kkc + 1) * 128],
                                        tri[:, :])
                                    ctx.inc(mk, "fl_mk")
                        ctx.wv(v, "dve", "fl_av", b_av + 2 * (u + 1) * (u + 2))
                        if u >= 1:
                            ctx.wv(v, "dve", f"d_ost{(u - 1) % 2}",
                                   b_ost[(u - 1) % 2] + 16 * ((u - 1) // 2 + 1))
                        else:
                            ctx.wv(v, "dve", "d_ost0", b_ost[0])
                            ctx.wv(v, "dve", "d_ost1", b_ost[1])
                        for q4 in range(4):
                            r = v.reciprocal(zr_col[:, q4:q4 + 1], ps[2 + q4][:, 128:129])
                            t1 = ctx.inc(r, "fl_fin")
                            ctx.wait(v, "dve", t1)
                            o = v.tensor_scalar(out=otok[:, :], in0=ps[2 + q4][:, 0:128],
                                                scalar1=zr_col[:, q4:q4 + 1], scalar2=None,
                                                op0=ALU.mult)
                            t2 = ctx.inc(o, "fl_fin")
                            ctx.wv(v, "dve", "fl_tr", b_tr + u * 4 + q4 + 1)
                            cp = v.tensor_copy(obuf[:, q4 * 128:(q4 + 1) * 128],
                                               ps[6][:, 0:128])
                            ctx.inc(cp, "fl_oe")

                @blk.sync
                def _(sp):
                    for u in range(16):
                        ctx.wv(sp, "sp", "fl_oe", b_oe + 4 * (u + 1))
                        ctx.inc(sp.dma_start(out=oag_in[hi][:, u * 512:(u + 1) * 512],
                                             in_=obuf[:, :]), f"d_ost{u % 2}", 16)

                @blk.gpsimd
                def _(g):
                    ctx.wait(g, "gps", ctx.tok("d_ost0"))
                    ctx.wait(g, "gps", ctx.tok("d_ost1"))
                    cc = g.collective_compute(
                        "AllGather", ALU.bypass, replica_groups=[list(range(NCORE))],
                        ins=[oag_in[hi].rearrange("a b -> (a b)")[:]],
                        outs=[oag_out[hi].rearrange("r a b -> (r a b)")[:]])
                    ctx.inc(cc, "ag_o")
                ctx.barrier()
            tok_ago = ctx.tok("ag_o")

            flsc.close()
            osc = Scope()
            ofm_all = osc.sbuf("ofm_all", [128, EC * TOK], BF16)
            qkslab = [osc.sbuf(f"oslab{i}", [128, EC * 128], BF16) for i in range(2)]

            @blk.gpsimd
            def _(g):
                ctx.wait(g, "gps", tok_ago)
                for arm in g.switch_core_id(128):
                    my = arm.logical % 8
                    for hh in range(H):
                        r, s_ = hh // 2, hh % 2
                        g.dma_start(
                            out=ofm_all[:, hh * TOK:(hh + 1) * TOK],
                            in_=oag_out[s_, r][:, my * TOK:(my + 1) * TOK]
                        ).then_inc(ctx.sem("d_ofm"), 16)
                ctx.counts[ctx._r("d_ofm")] += 16 * H
            tok_ofm = ctx.tok("d_ofm")

            b_ol = [ctx.n("d_wo0"), ctx.n("d_wo1")]
            b_ope = ctx.n("o_pe")
            b_oep = ctx.n("o_ep")

            @blk.gpsimd
            def _(g):
                for eo in range(EC):
                    if eo >= 2:
                        ctx.wv(g, "gps", "o_pe", b_ope + 2 * (eo - 1))
                    src = w_attn["wo"][:, eo * 128:(eo + 1) * 128]
                    ctx.inc(g.dma_start(
                        out=qkslab[eo % 2][:, :].rearrange("p (ic e) -> p ic e", e=128),
                        in_=src.rearrange("(ic p) e -> p ic e", p=128)),
                        f"d_wo{eo % 2}", 16)

            @blk.tensor
            def _(t):
                ctx.wait(t, "pe", tok_ofm)
                for eo in range(EC):
                    ctx.wv(t, "pe", f"d_wo{eo % 2}", b_ol[eo % 2] + 16 * (eo // 2 + 1))
                    for tt2 in range(2):
                        u = eo * 2 + tt2
                        if u >= 2:
                            t.wait_ge(ctx.sem("o_ep"), b_oep + u - 1)
                        for ic in range(EC):
                            mm = t.matmul(ps[u % 2][:, 0:TT],
                                          qkslab[eo % 2][:, ic * 128:(ic + 1) * 128],
                                          ofm_all[:, ic * TOK + tt2 * TT:
                                                  ic * TOK + tt2 * TT + TT],
                                          start=(ic == 0), stop=(ic == EC - 1))
                        ctx.inc(mm, "o_pe")

            @blk.vector
            def _(v):
                for eo in range(EC):
                    for tt2 in range(2):
                        u = eo * 2 + tt2
                        ctx.wv(v, "dve", "o_pe", b_ope + u + 1)
                        e = v.scalar_tensor_tensor(
                            out=x[eo][:, tt2 * TT:(tt2 + 1) * TT], in0=ps[u % 2][:, 0:TT],
                            scalar=bias_sb[:, 3 * EC + eo:3 * EC + eo + 1],
                            in1=x[eo][:, tt2 * TT:(tt2 + 1) * TT],
                            op0=ALU.add, op1=ALU.add)
                        ctx.inc(e, "o_ep")
            ctx.barrier()
            osc.close()

        if n_phases >= PH_FFN1 and FFN_STOP != "none":
            ffn_pass(1)
        if n_phases >= PH_ATTN:
            attn_phase()
        if n_phases >= PH_FFN2:
            ffn_pass(2)

        @blk.sync
        def _(sp):
            ctx.wait(sp, "sp", ctx.tok("dep"))
            ctx.wait(sp, "sp", ctx.tok("o_ep"))
            for i in range(EC):
                ctx.inc(sp.dma_start(out=y_out[128 * i:128 * (i + 1), :], in_=x[i][:]),
                        "d_y", 16)
            sp.wait_ge(ctx.sem("d_y"), ctx.counts["d_y"])

    return nc



_PERM = np.concatenate([np.arange(0, D, 2), np.arange(1, D, 2)])


def _rope_tables():
    pos = np.arange(S, dtype=np.float64)[:, None]
    exps = -np.arange(0, D, 2, dtype=np.float64) / D
    thetas = np.repeat(np.power(10000.0, exps), 2)
    ang = (pos * thetas[None, :])
    cos = np.cos(ang).astype(np.float32)
    sin = np.sin(ang).astype(np.float32)
    posf = np.arange(S, dtype=np.float32)[:, None]
    expsf = -np.arange(0, D, 2, dtype=np.float32) / np.float32(D)
    thetasf = np.repeat(np.power(np.float32(10000.0), expsf), 2)
    angf = posf * thetasf[None, :]
    return np.cos(angf).astype(np.float32), np.sin(angf).astype(np.float32)


def kernel(L0_attn_x, L0_attn_Wq, L0_attn_bq, L0_attn_Wk, L0_attn_bk,
           L0_attn_Wv, L0_attn_bv, L0_attn_Wo, L0_attn_bo,
           L0_ffn_w_gate, L0_ffn_w_up, L0_ffn_w_down):
    x = np.ascontiguousarray(np.asarray(L0_attn_x, dtype=np.float32).reshape(S, E))
    cos, sin = _rope_tables()

    perm_full = np.concatenate([h * D + _PERM for h in range(H)])
    Wq = np.ascontiguousarray(np.asarray(L0_attn_Wq, np.float32)[:, perm_full])
    Wk = np.ascontiguousarray(np.asarray(L0_attn_Wk, np.float32)[:, perm_full])
    bq = np.ascontiguousarray(np.asarray(L0_attn_bq, np.float32)[perm_full])[:, None]
    bk = np.ascontiguousarray(np.asarray(L0_attn_bk, np.float32)[perm_full])[:, None]
    Wv = np.ascontiguousarray(np.asarray(L0_attn_Wv, np.float32))
    bv = np.ascontiguousarray(np.asarray(L0_attn_bv, np.float32))[:, None]
    Wo = np.ascontiguousarray(np.asarray(L0_attn_Wo, np.float32))
    bo = np.ascontiguousarray(np.asarray(L0_attn_bo, np.float32))[:, None]

    wgT_f = np.ascontiguousarray(np.asarray(L0_ffn_w_gate, np.float32).T)
    wuT_f = np.ascontiguousarray(np.asarray(L0_ffn_w_up, np.float32).T)
    wdT_f = np.ascontiguousarray(np.asarray(L0_ffn_w_down, np.float32).T)

    cosp = cos[:, _PERM]
    sinp = sin[:, _PERM]
    sgn = np.ones(D, dtype=np.float32)
    sgn[: D // 2] = -1.0
    sins = sinp * sgn[None, :]

    tri_np = np.triu(np.ones((128, 128), np.float32)).astype(ml_dtypes.bfloat16)
    ident_np = np.eye(128, dtype=np.float32).astype(ml_dtypes.bfloat16)

    nc = build_program(N_PHASES)
    in_maps = []
    for c in range(NCORE):
        tsl = slice(c * TOK, (c + 1) * TOK)
        in_maps.append({
            "x_in": np.ascontiguousarray(x[tsl].T),
            "wq": Wq, "wk": Wk, "wv": Wv, "wo": Wo,
            "b_wq": bq, "b_wk": bk, "b_wv": bv, "b_wo": bo,
            "wgT": np.ascontiguousarray(wgT_f[:, tsl]),
            "wuT": np.ascontiguousarray(wuT_f[:, tsl]),
            "wdT": np.ascontiguousarray(wdT_f[tsl, :]),
            "cosr": np.ascontiguousarray(cosp[tsl].T),
            "sinr": np.ascontiguousarray(sins[tsl].T),
            "tri": tri_np,
            "ident": ident_np,
        })

    res = run_bass_kernel_spmd(nc, in_maps, core_ids=list(range(NCORE)))
    out = np.empty((S, E), dtype=np.float32)
    for c in range(NCORE):
        out[c * TOK:(c + 1) * TOK] = res.results[c]["y_out"].T
    return out.reshape(1, S, E)
